# version 17
# baseline (speedup 1.0000x reference)
import math
from contextlib import ExitStack

import numpy as np
import concourse.bass as bass
import concourse.mybir as mybir
from concourse.bass_utils import run_bass_kernel_spmd

F32 = mybir.dt.float32
BF16 = mybir.dt.bfloat16
U32 = mybir.dt.uint32
AF = mybir.ActivationFunctionType
TS = mybir.AluOpType
AX = mybir.AxisListType

NCORES = 8
ENGINES = ("t", "v", "s", "g", "q")
DMA_ENG = {"g", "q"}


class Op:
    __slots__ = ("eng", "emit", "deps", "inc", "idx", "tok", "sem", "ring", "prewait", "ind", "scope")

    def __init__(self, eng, emit, deps, inc):
        self.eng, self.emit, self.deps, self.inc = eng, emit, deps, inc


class Sched:

    RING = 20

    def __init__(self):
        self.ops = []
        self.last = {}
        self.scope = None

    def add(self, eng, emit, deps=(), inc=None, ind=False):
        if inc is None:
            inc = 16 if eng in DMA_ENG else 1
        if eng in DMA_ENG and inc == 1:
            ind = True
        deps = [d for d in deps if d is not None]
        op = Op(eng, emit, deps, inc)
        op.scope = self.scope
        op.ind = ind
        op.idx = len(self.ops)
        for d in deps:
            assert d.idx < op.idx, "deps must reference earlier ops"
        if ind is not False:
            chan = 0 if ind is True else int(ind)
            op.ind = chan if op.inc != 1 else "cc"
            lasts = getattr(self, "_last_ind", None)
            if lasts is None:
                lasts = self._last_ind = {}
            key = op.ind
            if lasts.get(key) is not None:
                op.deps = list(op.deps) + [lasts[key]]
            lasts[key] = op
        self.ops.append(op)
        self.last[eng] = op
        return op

    def barrier(self):
        return list(self.last.values())

    def emit_all(self, nc, ctx):
        import os
        lim = int(os.environ.get("SCHED_LIMIT", "0"))
        if lim > 0:
            self.ops = self.ops[:lim]
        sems = {e: ctx.enter_context(nc.semaphore(f"sem_{e}")) for e in ("t", "v", "s")}
        rings = {e: [ctx.enter_context(nc.semaphore(f"ring_{e}{i}"))
                     for i in range(self.RING)] for e in DMA_ENG}
        ind_chans = sorted({op.ind for op in self.ops
                            if op.ind is not False and op.ind != "cc"})
        ind_sems = {ch: ctx.enter_context(nc.semaphore(f"ind_sem{ch}"))
                    for ch in ind_chans}
        cc_sem = ctx.enter_context(nc.semaphore("cc_sem"))
        ind_tot = {ch: 0 for ch in ind_chans}
        cc_tot = 0
        counts = {e: 0 for e in ("t", "v", "s")}
        seq = {e: 0 for e in DMA_ENG}
        slot_tot = {e: [0] * self.RING for e in DMA_ENG}
        for op in self.ops:
            if op.ind is not False:
                if op.ind == "cc":
                    cc_tot += op.inc
                    op.sem = cc_sem
                    op.tok = cc_tot
                else:
                    ind_tot[op.ind] += op.inc
                    op.sem = ind_sems[op.ind]
                    op.tok = ind_tot[op.ind]
                op.ring = None
                op.prewait = 0
            elif op.eng in DMA_ENG:
                r = seq[op.eng] % self.RING
                seq[op.eng] += 1
                op.ring = r
                op.prewait = slot_tot[op.eng][r]
                slot_tot[op.eng][r] += op.inc
                op.sem = rings[op.eng][r]
                op.tok = slot_tot[op.eng][r]
            else:
                counts[op.eng] += op.inc
                op.sem = sems[op.eng]
                op.tok = counts[op.eng]
                op.ring = None
        block = ctx.enter_context(nc.Block())
        for ek in ENGINES:
            mine = [op for op in self.ops if op.eng == ek]
            if not mine:
                continue

            def section(engine, _mine=mine, _ek=ek):
                wm = {}
                cur_scope = [None, None]
                for op in _mine:
                    if op.scope != cur_scope[0]:
                        if cur_scope[0] is not None:
                            nc.leave_named_scope(cur_scope[0], cur_scope[1], False)
                        if op.scope is not None:
                            sid, _ = nc.enter_named_scope(op.scope, False)
                            cur_scope = [op.scope, sid]
                        else:
                            cur_scope = [None, None]
                    if op.ring is not None and op.prewait > 0:
                        key = (id(op.sem),)
                        if wm.get(key, -1) < op.prewait:
                            engine.wait_ge(op.sem, op.prewait)
                            wm[key] = op.prewait
                    for d in op.deps:
                        key = (id(d.sem),)
                        if wm.get(key, -1) < d.tok:
                            engine.wait_ge(d.sem, d.tok)
                            wm[key] = d.tok
                    ins = op.emit(engine)
                    ins.then_inc(op.sem, op.inc)
                if cur_scope[0] is not None:
                    nc.leave_named_scope(cur_scope[0], cur_scope[1], False)

            getattr(block, {"t": "tensor", "v": "vector", "s": "scalar",
                            "g": "gpsimd", "q": "sync"}[ek])(section)
        return sems, counts


def ceil_div(a, b):
    return (a + b - 1) // b


class Arena:

    def __init__(self, nc, ctx, nbytes, name="arena"):
        self.t = ctx.enter_context(nc.sbuf_tensor(name, [128, nbytes], mybir.dt.uint8))
        self.nbytes = nbytes
        self.off = 0
        self.marks = []

    def push(self):
        self.marks.append(self.off)

    def pop(self):
        self.off = self.marks.pop()

    def alloc(self, shape, dt):
        assert shape[0] <= 128
        n = int(np.prod(shape[1:])) * mybir.dt.size(dt)
        off = (self.off + 63) & ~63
        self.off = off + n
        assert self.off <= self.nbytes, f"arena overflow: {self.off} > {self.nbytes}"
        v = self.t[:, off:off + n].bitcast(dt)
        if len(shape) == 2:
            v = v[0:shape[0], :]
        elif len(shape) == 3:
            v = v.rearrange("p (a b) -> p a b", a=shape[1])[0:shape[0], :, :]
        elif len(shape) == 4:
            v = v.rearrange("p (a b c) -> p a b c", a=shape[1], b=shape[2])[0:shape[0]]
        else:
            raise ValueError(shape)
        return v



def build_l1(T=4096, D=1024, H=16, DK=64, E=16, CAP=256, NEG_BIAS=-16.0, BIS_ITERS=36, SINK_ITERS=6, DEBUG_LO=False):
    nc = bass.Bass("TRN2", target_bir_lowering=False, debug=False)
    HP, NT = H // 2, T // 128
    io = {}
    io["xT"] = nc.dram_tensor("xT", [D, T], F32, kind="ExternalInput").ap()
    io["wqkv_p"] = nc.dram_tensor("wqkv_p", [HP, D, 3 * 128], F32, kind="ExternalInput").ap()
    io["wproj"] = nc.dram_tensor("wproj", [D, D], F32, kind="ExternalInput").ap()
    io["gqT"] = nc.dram_tensor("gqT", [D, E], F32, kind="ExternalInput").ap()
    io["ltemp"] = nc.dram_tensor("ltemp", [1, 1], F32, kind="ExternalInput").ap()
    io["identf"] = nc.dram_tensor("identf", [128, 128], F32, kind="ExternalInput").ap()
    io["masks"] = nc.dram_tensor("masks", [128, 4, 512], BF16, kind="ExternalInput").ap()
    io["iota256"] = nc.dram_tensor("iota256", [128, 256], F32, kind="ExternalInput").ap()
    io["iotaN"] = nc.dram_tensor("iotaN", [128, NT], F32, kind="ExternalInput").ap()
    io["iotaNhi"] = nc.dram_tensor("iotaNhi", [128, NT], BF16, kind="ExternalInput").ap()
    io["iotaNlo"] = nc.dram_tensor("iotaNlo", [128, NT], BF16, kind="ExternalInput").ap()
    io["ones_in"] = nc.dram_tensor("ones_in", [128, 128], F32, kind="ExternalInput").ap()
    io["xmid_f"] = nc.dram_tensor("xmid_f", [T, D], F32, kind="ExternalOutput").ap()
    io["xmid_b"] = nc.dram_tensor("xmid_b", [T, D], BF16, kind="ExternalOutput").ap()
    io["idx_o"] = nc.dram_tensor("idx_o", [E, CAP], U32, kind="ExternalOutput").ap()
    io["gate_o"] = nc.dram_tensor("gate_o", [E, CAP], BF16 if False else F32, kind="ExternalOutput").ap()
    oacc_t = nc.dram_tensor("oacc_h", [HP, 128, T], BF16).ap()
    io["oacc_h"] = [oacc_t[hp].rearrange("p (j f) -> p j f", f=512)
                    for hp in range(HP)]
    io["oacc_flat"] = oacc_t.rearrange("h p t -> p h t")
    sch = Sched()
    with ExitStack() as ctx:
        sb = lambda name, shape, dt: ctx.enter_context(nc.sbuf_tensor(name, shape, dt))
        ps = mk_psum(nc, ctx)
        st = emit_l1(nc, sch, sb, ps, io, T=T, D=D, H=H, E=E, CAP=CAP,
                     NEG_BIAS=NEG_BIAS, BIS_ITERS=BIS_ITERS, SINK_ITERS=SINK_ITERS)
        fin = sch.add("g", lambda e: e.nop(), inc=1, deps=st["ln_dma"] + st["odl"])
        sch.emit_all(nc, ctx)
    return nc


def mk_psum(nc, ctx):
    wa = ctx.enter_context(nc.psum_tensor("pswa", [128, 1024], F32))
    wb = ctx.enter_context(nc.psum_tensor("pswb", [128, 1024], F32))
    ps = [wa[:, 0:512], wa[:, 512:1024], wb[:, 0:512], wb[:, 512:1024]]
    for i in range(4, 8):
        ps.append(ctx.enter_context(nc.psum_tensor(f"ps{i}", [128, 512], F32))[:, :])
    ps.append(wa[:, :])
    ps.append(wb[:, :])
    return ps


def emit_l1(nc, sch, sb, ps, io, T=4096, D=1024, H=16, E=16, CAP=256,
            NEG_BIAS=-16.0, BIS_ITERS=36, SINK_ITERS=6,
            HPL=None, oag=None, oall_rows=None, sel_experts=None):
    assert D % 128 == 0 and T % 512 == 0 and H % 2 == 0 and E == 16
    DC = D // 128
    HP = H // 2
    NT = T // 128
    if HPL is None:
        HPL = HP
    EH = E if sel_experts is None else sel_experts
    NJ = T // 512
    NSL = ceil_div(CAP, 128)
    assert NT * E <= 512, "sinkhorn free dim must fit one op"
    xT = io["xT"]; wqkv_p = io["wqkv_p"]; wproj = io["wproj"]; gqT = io["gqT"]
    ltemp = io["ltemp"]; identf = io["identf"]; masks = io["masks"]
    iota256 = io["iota256"]; iotaN = io["iotaN"]; ones_in = io["ones_in"]
    iotaNhi = io["iotaNhi"]; iotaNlo = io["iotaNlo"]
    xmid_f = io["xmid_f"]; xmid_b = io["xmid_b"]; idx_o = io["idx_o"]
    gate_o = io["gate_o"]; oacc_h = io["oacc_h"]
    oacc_flat = io.get("oacc_flat"); osel_i = io.get("osel")
    if True:
        xTb = sb("xTb", [128, DC, T], BF16)
        wqp = sb("wqp", [128, DC, 384], BF16)
        kT = sb("kT", [128, T], BF16)
        qT = sb("qT", [128, T], BF16)
        vsb = sb("vsb", [128, NT, 130], BF16)
        attb = sb("attb", [128, 2, 1024], BF16)
        zb = sb("zb", [65, 512], F32)
        zbc = sb("zbc", [128, 512], F32)
        otile = sb("otile", [128, 2, 512], BF16)
        wpj = sb("wpj", [128, DC, D], BF16)
        if T >= 3072:
            oin_v = lambda db: xTb[:, :, 2048 + 512 * db:2048 + 512 * db + 512]
            oin_c = lambda db, cc: xTb[:, cc, 2048 + 512 * db:2048 + 512 * db + 512]
            xf_v = lambda db: xTb[:, :, :].bitcast(F32)[:, :, 512 * db:512 * db + 512]
            xf_c = lambda db, c: xTb[:, c, :].bitcast(F32)[:, 512 * db:512 * db + 512]
        else:
            oin_t = sb("oin_t", [128, 2, DC, 512], BF16)
            xf_t = sb("xf_t", [128, 2, DC, 512], F32)
            oin_v = lambda db: oin_t[:, db, :, :]
            oin_c = lambda db, cc: oin_t[:, db, cc, :]
            xf_v = lambda db: xf_t[:, db, :, :]
            xf_c = lambda db, c: xf_t[:, db, c, :]
        gq_s = sb("gq_s", [128, DC, E], F32)
        rT = sb("rT", [128, DC, 512], F32)
        rrow = sb("rrow", [128, 2, D], F32)
        xrowb = sb("xrowb", [128, 2, D], BF16)
        stats = sb("stats", [128, 8], F32)
        logit = sb("logit", [128, NT, E], F32)
        esb = sb("esb", [128, NT * E], F32)
        sqs = sb("sqs", [128, D], F32)
        row1 = sb("row1", [1, 512], F32)
        red16 = sb("red16", [1, 16], F32)
        bc16 = sb("bc16", [128, 16], F32)
        g0bc = sb("g0bc", [128, E], F32)
        ltc = sb("ltc", [128, 1], F32)
        lo_t = sb("lo_t", [128, 16], F32)
        hi_t = sb("hi_t", [128, 16], F32)
        mid_t = sb("mid_t", [128, 16], F32)
        cmpb = sb("cmpb", [128, NT * E], BF16)
        mtok = sb("mtok", [128, NT, E], F32)
        if T >= 3072:
            maskT = xTb[:, 4:6, :].bitcast(F32).rearrange("p a b -> p (a b)")[0:16, 0:T]
            c0 = xTb[:, 6:8, :].bitcast(F32).rearrange("p a b -> p (a b)")[0:16, 0:T]
        else:
            maskT = sb("maskT", [16, T], F32)
            c0 = sb("c0", [16, T], F32)
        ctok = sb("ctok", [128, NT, E], F32)
        rhs35 = sb("rhs35", [128, NT, 35], BF16)
        ohb = sb("ohb", [128, 2, 256], BF16)
        ohsb = sb("ohsb", [35, 2, 256], F32)
        pyb = sb("pyb", [128, 2, 35], F32)
        ahib = sb("ahib", [128, NT * E], BF16)
        ioNhl = sb("ioNhl", [128, 2, NT], BF16)
        smallf = sb("smallf", [128, 4], F32)
        idx_sb = sb("idx_sb", [128, E, NSL], U32)
        gate_sb = sb("gate_sb", [128, E, NSL], F32)
        identsb = sb("identsb", [128, 128], F32)
        if oall_rows is not None:
            osel_sb = sb("osel_sb", [128, DC, NJ], U32)
        masksb = sb("masksb", [128, 4, 512], BF16)
        io256 = sb("io256", [128, 256], F32)
        onescf = sb("onescf", [128, 128], F32)
        onescb = sb("onescb", [128, 1], BF16)
        negb = sb("negb", [128, 1], F32)
        A = {}
        A["ident"] = sch.add("q", lambda e: e.dma_start(out=identsb[:, :], in_=identf[:, :]))
        A["masks"] = sch.add("q", lambda e: e.dma_start(out=masksb[:, :, :], in_=masks[:, :, :]))
        A["io256"] = sch.add("q", lambda e: e.dma_start(out=io256[:, :], in_=iota256[:, :]))
        A["ones"] = sch.add("q", lambda e: e.dma_start(out=onescf[:, :], in_=ones_in[:, :]))
        A["gq"] = sch.add("q", lambda e: e.dma_start(
            out=gq_s[:, :, :], in_=gqT.rearrange("(c p) e -> p c e", p=128)))
        A["lt"] = sch.add("q", lambda e: e.dma_start(out=stats[0:1, 7:8], in_=ltemp[:, :]))
        A["osel"] = (sch.add("q", lambda e: e.dma_start(out=osel_sb[:, :, :],
                                                        in_=osel_i[:, :, :]))
                     if oall_rows is not None else None)
        A["onescb"] = sch.add("v", lambda e: e.memset(onescb[:, :], 1.0), deps=())
        A["negb"] = sch.add("v", lambda e: e.memset(negb[:, :], NEG_BIAS), deps=())
        xtb_deps = []
        for c in range(DC):
            xtb_deps.append(sch.add("g", lambda e, c=c: e.dma_start(
                out=xTb[:, c, :], in_=xT[128 * c:128 * c + 128, :])))
        A["xTb"] = xtb_deps
        wpj_deps = []
        for c in range(DC):
            wpj_deps.append(sch.add("g", lambda e, c=c: e.dma_start(
                out=wpj[:, c, :], in_=wproj[128 * c:128 * c + 128, :])))
        A["wpj"] = wpj_deps
        A["vones"] = [
            sch.add("v", lambda e: e.memset(vsb[:, :, 64:65], 1.0)),
            sch.add("v", lambda e: e.memset(vsb[:, :, 129:130], 1.0)),
        ]
        elt = sch.add("s", lambda e: e.activation(out=stats[0:1, 6:7], in_=stats[0:1, 7:8],
                                                  func=AF.Exp, scale=-1.0), deps=[A["lt"]])
        eltm = sch.add("t", lambda e: e.matmul(ps[7][0:128, 0:1], onescf[0:1, 0:128],
                                               stats[0:1, 6:7], start=True, stop=True),
                       deps=[elt, A["ones"]])
        A["ltc"] = sch.add("v", lambda e: e.tensor_copy(out=ltc[:, :], in_=ps[7][0:128, 0:1]),
                           deps=[eltm])
        g0acc = None
        for c in range(DC):
            g0acc = sch.add("t", lambda e, c=c, s=(c == 0), p=(c == DC - 1): e.matmul(
                ps[6][0:1, 0:E], onescf[:, 0:1], gq_s[:, c, :], start=s, stop=p),
                deps=[A["gq"], A["ones"]] + ([A["ltc"]] if c == 0 else []))
        g0row = sch.add("v", lambda e: e.tensor_copy(out=red16[:, :], in_=ps[6][0:1, 0:E]),
                        deps=[g0acc])
        g0m = sch.add("t", lambda e: e.matmul(ps[6][0:128, 0:E], onescf[0:1, 0:128],
                                              red16[0:1, :], start=True, stop=True),
                      deps=[g0row])
        A["g0bc"] = sch.add("v", lambda e: e.tensor_copy(out=g0bc[:, :], in_=ps[6][0:128, 0:E]),
                            deps=[g0m])

        base_scope = sch.scope
        otile_rd = [None, None]
        kqv_last_use = None
        ps_user = [None] * 8
        ps_user[6] = A["g0bc"]
        ps_user[7] = A["ltc"]
        att_rd = [None, None]
        oacc_wr = {}

        ag_ops = {}
        for hp in range(HPL):
            sch.scope = f"{base_scope}_qkv"
            wq_dep = sch.add("g", lambda e, hp=hp: e.dma_start(
                out=wqp[:, :, :],
                in_=wqkv_p[hp, :, :].rearrange("(c p) w -> p c w", p=128)),
                deps=[kqv_last_use] if kqv_last_use else [])
            k_ev = {}
            q_ev = {}
            v_ev = {}
            for tj8 in range(NJ):
                b = tj8 % 2
                acc = None
                for c in range(DC):
                    acc = sch.add("t", lambda e, c=c, tj8=tj8, b=b, s=(c == 0), p=(c == DC - 1):
                                  e.matmul(ps[b][0:128, :], wqp[:, c, 128:256],
                                           xTb[:, c, 512 * tj8:512 * tj8 + 512],
                                           start=s, stop=p),
                                  deps=([wq_dep] + A["xTb"] + ([ps_user[b]] if ps_user[b] else [])) if c == 0 else [])
                k_ev[tj8] = sch.add("v", lambda e, tj8=tj8, b=b: e.tensor_copy(
                    out=kT[:, 512 * tj8:512 * tj8 + 512], in_=ps[b][0:128, :]), deps=[acc])
                ps_user[b] = k_ev[tj8]
            for tj8 in range(NJ):
                b = tj8 % 2
                acc = None
                for c in range(DC):
                    acc = sch.add("t", lambda e, c=c, tj8=tj8, b=b, s=(c == 0), p=(c == DC - 1):
                                  e.matmul(ps[b][0:128, :], wqp[:, c, 0:128],
                                           xTb[:, c, 512 * tj8:512 * tj8 + 512],
                                           start=s, stop=p),
                                  deps=[ps_user[b]] if c == 0 else [])
                q_ev[tj8] = sch.add("v", lambda e, tj8=tj8, b=b: e.tensor_copy(
                    out=qT[:, 512 * tj8:512 * tj8 + 512], in_=ps[b][0:128, :]), deps=[acc])
                ps_user[b] = q_ev[tj8]
            for sbk in range(NT):
                b = sbk % 2
                acc = None
                for c in range(DC):
                    acc = sch.add("t", lambda e, c=c, sbk=sbk, b=b, s=(c == 0), p=(c == DC - 1):
                                  e.matmul(ps[b][0:128, 0:128], xTb[:, c, 128 * sbk:128 * sbk + 128],
                                           wqp[:, c, 256:384], start=s, stop=p),
                                  deps=[ps_user[b]] if c == 0 else [])
                ev0 = sch.add("v", lambda e, sbk=sbk, b=b: e.tensor_copy(
                    out=vsb[:, sbk, 0:64], in_=ps[b][0:128, 0:64]),
                    deps=[acc] + (A["vones"] if sbk == 0 else []))
                ev1 = sch.add("v", lambda e, sbk=sbk, b=b: e.tensor_copy(
                    out=vsb[:, sbk, 65:129], in_=ps[b][0:128, 64:128]), deps=[acc])
                v_ev[sbk] = ev1
                ps_user[b] = ev1

            sch.scope = f"{base_scope}_att"
            for tj in range(NJ):
                nsb = 4 * (tj + 1)

                def emit_S(si, tj=tj):
                    par = si % 2
                    kdep = k_ev[max(0, (128 * si + 127) // 512)]
                    mm_e = sch.add("t", lambda e, si=si, tj=tj, par=par: e.matmul(
                        ps[2 * par][0:128, :], kT[0:64, 128 * si:128 * si + 128],
                        qT[0:64, 512 * tj:512 * tj + 512], start=True, stop=True),
                        deps=[kdep, q_ev[tj], ps_user[2 * par]])
                    mm_o = sch.add("t", lambda e, si=si, tj=tj, par=par: e.matmul(
                        ps[2 * par + 1][0:128, :], kT[64:128, 128 * si:128 * si + 128],
                        qT[64:128, 512 * tj:512 * tj + 512], start=True, stop=True),
                        deps=[kdep, ps_user[2 * par + 1]])
                    return [mm_e, mm_o]

                pend = {0: emit_S(0)}
                avz_first = True
                for si in range(nsb):
                    par = si % 2
                    if si + 1 < nsb:
                        pend[si + 1] = emit_S(si + 1)
                    mm = pend.pop(si)
                    ex = sch.add("s", lambda e, par=par: e.activation(
                        out=attb[:, par, :], in_=ps[8 + par][0:128, :], func=AF.Exp,
                        scale=0.125, bias=negb[:, 0:1]),
                        deps=mm + [att_rd[par], A["negb"]])
                    ps_user[2 * par] = ex
                    ps_user[2 * par + 1] = ex
                    if si >= 4 * tj:
                        r = si - 4 * tj
                        ex = sch.add("v", lambda e, par=par, r=r: e.tensor_mul(
                            attb[:, par, :].rearrange("p (h f) -> p h f", h=2),
                            attb[:, par, :].rearrange("p (h f) -> p h f", h=2),
                            masksb[:, r, :].unsqueeze(1).to_broadcast([128, 2, 512])),
                            deps=[ex, A["masks"]])
                    st = avz_first
                    fin_ = (si == nsb - 1)
                    av_e = sch.add("t", lambda e, si=si, par=par, st=st, fin_=fin_: e.matmul(
                        ps[4][0:65, :], vsb[:, si, 0:65], attb[:, par, 0:512],
                        start=st, stop=fin_), deps=[ex, v_ev[si]] +
                        ([ps_user[4]] if st and ps_user[4] else []))
                    av_o = sch.add("t", lambda e, si=si, par=par, st=st, fin_=fin_: e.matmul(
                        ps[5][64:128, :], vsb[:, si, 65:129], attb[:, par, 512:1024],
                        start=st, stop=fin_, tile_position=(0, 64)), deps=[ex] +
                        ([ps_user[5]] if st and ps_user[5] else []))
                    z_o = sch.add("t", lambda e, si=si, par=par, st=st, fin_=fin_: e.matmul(
                        ps[7][32:33, :], vsb[:, si, 129:130], attb[:, par, 512:1024],
                        start=st, stop=fin_, tile_position=(0, 32)), deps=[ex] +
                        ([ps_user[7]] if st and ps_user[7] else []))
                    avz_first = False
                    att_rd[par] = z_o
                    av_last = [av_e, av_o, z_o]
                cz0 = sch.add("v", lambda e: e.tensor_copy(out=zb[64:65, :],
                                                            in_=ps[4][64:65, :]),
                              deps=av_last)
                cz1 = sch.add("v", lambda e: e.tensor_copy(out=zb[32:33, :], in_=ps[7][32:33, :]),
                              deps=av_last)
                ps_user[7] = cz1
                r0 = sch.add("v", lambda e: e.reciprocal(zb[64:65, :], zb[64:65, :]), deps=[cz0])
                r1 = sch.add("v", lambda e: e.reciprocal(zb[32:33, :], zb[32:33, :]), deps=[cz1])
                b0 = sch.add("t", lambda e: e.matmul(ps[0][0:64, :], onescf[64:65, 0:64],
                                                     zb[64:65, :], start=True, stop=True,
                                                     tile_position=(64, 0)),
                             deps=[r0, A["ones"], ps_user[0]])
                b1 = sch.add("t", lambda e: e.matmul(ps[0][64:128, :], onescf[32:33, 0:64],
                                                     zb[32:33, :], start=True, stop=True,
                                                     tile_position=(32, 64)), deps=[r1])
                zev = sch.add("v", lambda e: e.tensor_copy(out=zbc[:, :], in_=ps[0][0:128, :]),
                              deps=[b0, b1])
                ps_user[0] = zev
                ob = tj % 2
                on1 = sch.add("v", lambda e, ob=ob: e.tensor_mul(
                    otile[0:64, ob, :], ps[4][0:64, :], zbc[0:64, :]),
                    deps=[zev] + ([otile_rd[ob]] if otile_rd[ob] else []))
                on2 = sch.add("v", lambda e, ob=ob: e.tensor_mul(
                    otile[64:128, ob, :], ps[5][64:128, :], zbc[64:128, :]),
                    deps=[zev, on1])
                ps_user[4] = on1
                ps_user[5] = on2
                odma = sch.add("q", lambda e, hp=hp, tj=tj, ob=ob: e.dma_start(
                    out=oacc_h[hp][:, tj, :], in_=otile[:, ob, :]),
                    deps=[on1, on2])
                otile_rd[ob] = odma
                oacc_wr[(hp, tj)] = odma
            kqv_last_use = av_last[-1]
            if oag is not None:
                ag_ops[hp] = oag(hp, [oacc_wr[(hp, tj)] for tj in range(NJ)])


        sch.scope = f"{base_scope}_projln"
        lastC = kqv_last_use
        oin_rd = [None, None]
        xf_rd = [None, None]
        rT_rd = None
        rrow_rd = [None, None]
        xrowb_rd = [None, None]
        stats_rd = None
        logit_wr = []
        ln_dma = []
        for tj in range(NJ):
            db = tj % 2
            if oall_rows is None:
                oin_ld = [sch.add("q", lambda e, tj=tj, db=db: e.dma_start(
                    out=oin_v(db),
                    in_=oacc_flat[:, :, 512 * tj:512 * tj + 512]),
                    deps=[oacc_wr[(hp2, tj)] for hp2 in range(HPL)] + [lastC] +
                         ([oin_rd[db]] if oin_rd[db] else []))] * DC
            else:
                oin_ld = []
                for cc in range(DC):
                    oin_ld.append(sch.add("g", lambda e, tj=tj, db=db, cc=cc:
                                          e.indirect_dma_start(
                        out=oin_c(db, cc), out_offset=None,
                        in_=oall_rows[cc % HPL],
                        in_offset=bass.IndirectOffsetOnAxis(
                            osel_sb[:, cc, tj:tj + 1], 0)),
                        deps=[ag_ops[cc % HPL], A["osel"]] +
                             ([oin_rd[db]] if oin_rd[db] and cc == 0 else [])))
            xf_ld = sch.add("q", lambda e, tj=tj, db=db: e.dma_start(
                out=xf_v(db),
                in_=xT.rearrange("(c p) t -> p c t", p=128)[:, :, 512 * tj:512 * tj + 512]),
                deps=[lastC] + ([xf_rd[db]] if xf_rd[db] else []))
            rT_parts = []
            for c in range(DC):
                b = c % 2
                acc = None
                for cc in range(DC):
                    acc = sch.add("t", lambda e, c=c, cc=cc, b=b, db=db, s=(cc == 0), p=(cc == DC - 1):
                                  e.matmul(ps[b][0:128, :], wpj[:, cc, 128 * c:128 * c + 128],
                                           oin_c(db, cc),
                                           start=s, stop=p),
                                  deps=([oin_ld[cc], ps_user[b]] + A["wpj"] +
                                        ([lastC] if lastC else [])) if cc == 0
                                  else [oin_ld[cc]])
                radd = sch.add("v", lambda e, c=c, b=b, db=db: e.tensor_add(
                    rT[:, c, :], ps[b][0:128, :], xf_c(db, c)),
                    deps=[acc, xf_ld] + ([rT_rd] if rT_rd and c == 0 else []))
                ps_user[b] = radd
                rT_parts.append(radd)
            for sub in range(4):
                rb = sub % 2
                l0acc = None
                for c in range(DC):
                    l0acc = sch.add("t", lambda e, c=c, sub=sub, s=(c == 0), p=(c == DC - 1):
                                    e.matmul(ps[2][0:128, 0:E], rT[:, c, 128 * sub:128 * sub + 128],
                                             gq_s[:, c, :], start=s, stop=p),
                                    deps=(rT_parts + [ps_user[2], A["gq"]]) if c == 0 else [])
                tp_ev = []
                for c in range(DC):
                    tp = sch.add("t", lambda e, c=c, sub=sub: e.transpose(
                        ps[3][0:128, 0:128], rT[:, c, 128 * sub:128 * sub + 128], identsb[:, :]),
                        deps=[l0acc, A["ident"]] if c == 0 else [ps_user[3]])
                    ev = sch.add("v", lambda e, c=c, rb=rb: e.tensor_copy(
                        out=rrow[:, rb, 128 * c:128 * c + 128], in_=ps[3][0:128, 0:128]),
                        deps=[tp] + ([rrow_rd[rb]] if rrow_rd[rb] and c == 0 else []))
                    ps_user[3] = ev
                    tp_ev.append(ev)
                c1 = sch.add("v", lambda e, rb=rb: e.reduce_sum(
                    stats[:, 0:1], rrow[:, rb, :], axis=AX.X),
                    deps=tp_ev + ([stats_rd] if stats_rd else []))
                c2 = sch.add("v", lambda e: e.tensor_scalar(
                    out=stats[:, 0:1], in0=stats[:, 0:1], scalar1=1.0 / D, scalar2=None,
                    op0=TS.mult), deps=[c1])
                sqv = sch.add("v", lambda e, rb=rb: e.tensor_mul(
                    sqs[:, :], rrow[:, rb, :], rrow[:, rb, :]),
                    deps=tp_ev + ([stats_rd] if stats_rd else []))
                c4 = sch.add("v", lambda e: e.reduce_sum(
                    stats[:, 1:2], sqs[:, :], axis=AX.X), deps=[sqv])
                c5 = sch.add("v", lambda e: e.tensor_scalar(
                    out=stats[:, 1:2], in0=stats[:, 1:2], scalar1=1.0 / D, scalar2=None,
                    op0=TS.mult), deps=[c4])
                c6 = sch.add("v", lambda e: e.tensor_mul(
                    stats[:, 2:3], stats[:, 0:1], stats[:, 0:1]), deps=[c2])
                c7 = sch.add("v", lambda e: e.tensor_tensor(
                    out=stats[:, 1:2], in0=stats[:, 1:2], in1=stats[:, 2:3], op=TS.subtract),
                    deps=[c5, c6])
                c8 = sch.add("v", lambda e: e.tensor_scalar(
                    out=stats[:, 1:2], in0=stats[:, 1:2], scalar1=1e-5, scalar2=None,
                    op0=TS.add), deps=[c7])
                c9 = sch.add("s", lambda e: e.activation(
                    out=stats[:, 3:4], in_=stats[:, 1:2], func=AF.Sqrt), deps=[c8])
                c10 = sch.add("v", lambda e: e.reciprocal(stats[:, 4:5], stats[:, 3:4]),
                              deps=[c9])
                xm = sch.add("v", lambda e, rb=rb: e.tensor_scalar(
                    out=rrow[:, rb, :], in0=rrow[:, rb, :], scalar1=stats[:, 0:1],
                    scalar2=stats[:, 4:5], op0=TS.subtract, op1=TS.mult),
                    deps=[c10, sqv])
                xb = sch.add("v", lambda e, rb=rb: e.tensor_copy(
                    out=xrowb[:, rb, :], in_=rrow[:, rb, :]),
                    deps=[xm] + ([xrowb_rd[rb]] if xrowb_rd[rb] else []))
                row0 = 512 * tj + 128 * sub
                d1 = sch.add("q", lambda e, rb=rb, row0=row0: e.dma_start(
                    out=xmid_f[row0:row0 + 128, :], in_=rrow[:, rb, :]), deps=[xm])
                d2 = sch.add("q", lambda e, rb=rb, row0=row0: e.dma_start(
                    out=xmid_b[row0:row0 + 128, :], in_=xrowb[:, rb, :]), deps=[xb])
                ln_dma += [d1, d2]
                rrow_rd[rb] = d1
                xrowb_rd[rb] = d2
                lg1 = sch.add("v", lambda e: e.tensor_scalar(
                    out=bc16[:, :], in0=g0bc[:, :], scalar1=stats[:, 0:1], scalar2=None,
                    op0=TS.mult), deps=[c10, A["g0bc"]])
                lg2 = sch.add("v", lambda e, tj=tj, sub=sub: e.tensor_tensor(
                    out=logit[:, 4 * tj + sub, :], in0=ps[2][0:128, 0:E], in1=bc16[:, :],
                    op=TS.subtract), deps=[lg1, l0acc])
                ps_user[2] = lg2
                lg3 = sch.add("v", lambda e: e.tensor_mul(
                    stats[:, 5:6], stats[:, 4:5], ltc[:, :]), deps=[c10, A["ltc"]])
                lg4 = sch.add("v", lambda e, tj=tj, sub=sub: e.tensor_scalar(
                    out=logit[:, 4 * tj + sub, :], in0=logit[:, 4 * tj + sub, :],
                    scalar1=stats[:, 5:6], scalar2=None, op0=TS.mult), deps=[lg2, lg3])
                logit_wr.append(lg4)
                stats_rd = lg4
            oin_rd[db] = rT_parts[-1]
            xf_rd[db] = rT_parts[-1]
            rT_rd = ps_user[3]

        sch.scope = f"{base_scope}_sink"
        prev = logit_wr[:]
        lgview = logit
        for it in range(SINK_ITERS):
            ex1 = sch.add("s", lambda e: e.activation(
                out=esb[:, 0:NT * E], in_=lgview.rearrange("p n e -> p (n e)")[:, :],
                func=AF.Exp), deps=prev)
            red1 = sch.add("v", lambda e: e.reduce_sum(
                mtok[:, :, 0:1], esb.rearrange("p (n e) -> p n e", e=E)[:, :, :],
                axis=AX.X), deps=[ex1])
            ln1 = sch.add("s", lambda e: e.activation(
                out=mtok[:, :, 1:2], in_=mtok[:, :, 0:1], func=AF.Ln), deps=[red1])
            sub1 = sch.add("v", lambda e: e.tensor_tensor(
                out=lgview[:, :, :], in0=lgview[:, :, :],
                in1=mtok[:, :, 1:2].to_broadcast([128, NT, E]), op=TS.subtract),
                deps=[ln1])
            ex2 = sch.add("s", lambda e: e.activation(
                out=esb[:, 0:NT * E], in_=lgview.rearrange("p n e -> p (n e)")[:, :],
                func=AF.Exp), deps=[sub1])
            cs1 = sch.add("t", lambda e: e.matmul(
                ps[2][0:1, 0:NT * E], onescf[:, 0:1], esb[:, 0:NT * E],
                start=True, stop=True), deps=[ex2, ps_user[2]])
            ev1 = sch.add("v", lambda e: e.tensor_copy(out=row1[:, 0:NT * E],
                                                       in_=ps[2][0:1, 0:NT * E]), deps=[cs1])
            ps_user[2] = ev1
            red2 = sch.add("v", lambda e: e.reduce_sum(
                red16[:, :].unsqueeze(2),
                row1[:, 0:NT * E].rearrange("a (n e) -> a e n", e=E)[:, :, :], axis=AX.X), deps=[ev1])
            ln2 = sch.add("s", lambda e: e.activation(
                out=row1[0:1, 0:E], in_=red16[:, :], func=AF.Ln), deps=[red2])
            bc1 = sch.add("t", lambda e: e.matmul(
                ps[2][0:128, 0:E], onescf[0:1, 0:128], row1[0:1, 0:E],
                start=True, stop=True), deps=[ln2, ps_user[2]])
            ev2 = sch.add("v", lambda e: e.tensor_copy(out=bc16[:, 0:E],
                                                       in_=ps[2][0:128, 0:E]), deps=[bc1])
            ps_user[2] = ev2
            sub2 = sch.add("v", lambda e: e.tensor_tensor(
                out=lgview[:, :, :], in0=lgview[:, :, :],
                in1=bc16[:, :].unsqueeze(1).to_broadcast([128, NT, E]), op=TS.subtract),
                deps=[ev2])
            prev = [sub2]
        sink_done = prev[0]
        aff = sch.add("s", lambda e: e.activation(
            out=esb[:, 0:NT * E], in_=lgview.rearrange("p n e -> p (n e)")[:, :],
            func=AF.Exp), deps=[sink_done])

        sch.scope = f"{base_scope}_bis"
        i_lo = sch.add("v", lambda e: e.memset(lo_t[:, :], -60.0), deps=[sink_done])
        i_hi = sch.add("v", lambda e: e.memset(hi_t[:, :], 0.0), deps=[sink_done])
        pv = [i_lo, i_hi]
        for it in range(BIS_ITERS):
            m1 = sch.add("v", lambda e: e.tensor_add(mid_t[:, :], lo_t[:, :], hi_t[:, :]),
                         deps=pv)
            m2 = sch.add("v", lambda e: e.tensor_scalar(
                out=mid_t[:, :], in0=mid_t[:, :], scalar1=0.5, scalar2=None, op0=TS.mult),
                deps=[m1])
            cp = sch.add("v", lambda e: e.tensor_tensor(
                out=cmpb.rearrange("p (n e) -> p n e", e=E)[:, :, :], in0=lgview[:, :, :],
                in1=mid_t[:, :].unsqueeze(1).to_broadcast([128, NT, E]), op=TS.is_gt), deps=[m2])
            ct = sch.add("t", lambda e: e.matmul(
                ps[2][0:1, 0:NT * E], onescb[:, 0:1], cmpb[:, 0:NT * E],
                start=True, stop=True), deps=[cp, ps_user[2], A["onescb"]])
            ev = sch.add("v", lambda e: e.tensor_copy(out=row1[:, 0:NT * E],
                                                      in_=ps[2][0:1, 0:NT * E]), deps=[ct])
            ps_user[2] = ev
            rd = sch.add("v", lambda e: e.reduce_sum(
                red16[:, :].unsqueeze(2),
                row1[:, 0:NT * E].rearrange("a (n e) -> a e n", e=E)[:, :, :], axis=AX.X), deps=[ev])
            cg = sch.add("v", lambda e: e.tensor_scalar(
                out=red16[:, :], in0=red16[:, :], scalar1=float(CAP) - 0.5, scalar2=None,
                op0=TS.is_gt), deps=[rd])
            bc = sch.add("t", lambda e: e.matmul(
                ps[2][0:128, 0:E], onescf[0:1, 0:128], red16[0:1, :],
                start=True, stop=True), deps=[cg, ps_user[2]])
            ev2 = sch.add("v", lambda e: e.tensor_copy(out=bc16[:, 0:E],
                                                       in_=ps[2][0:128, 0:E]), deps=[bc])
            ps_user[2] = ev2
            ncge = sch.add("v", lambda e: e.tensor_scalar(
                out=mtok[:, 3, :], in0=bc16[:, 0:E], scalar1=-1.0,
                scalar2=1.0, op0=TS.mult, op1=TS.add), deps=[ev2])
            t1 = sch.add("v", lambda e: e.tensor_mul(mtok[:, 0, :], mid_t[:, :], bc16[:, 0:E]),
                         deps=[ev2])
            t2 = sch.add("v", lambda e: e.tensor_mul(mtok[:, 1, :], lo_t[:, :], mtok[:, 3, :]),
                         deps=[ncge])
            s1 = sch.add("v", lambda e: e.tensor_add(lo_t[:, :], mtok[:, 0, :], mtok[:, 1, :]),
                         deps=[t1, t2])
            t3 = sch.add("v", lambda e: e.tensor_mul(mtok[:, 2, :], hi_t[:, :], bc16[:, 0:E]),
                         deps=[ev2, t2])
            t4 = sch.add("v", lambda e: e.tensor_mul(mtok[:, 1, :], mid_t[:, :], mtok[:, 3, :]),
                         deps=[s1])
            s2 = sch.add("v", lambda e: e.tensor_add(hi_t[:, :], mtok[:, 2, :], mtok[:, 1, :]),
                         deps=[t3, t4])
            pv = [s1, s2]
        bis_done = pv

        sch.scope = f"{base_scope}_sel"
        g1 = sch.add("v", lambda e: e.tensor_tensor(
            out=mtok[:, :, :], in0=lgview[:, :, :],
            in1=lo_t[:, :].unsqueeze(1).to_broadcast([128, NT, E]), op=TS.is_gt),
            deps=bis_done + [sink_done])
        mt_ev = []
        for ntk in range(NT):
            tp = sch.add("t", lambda e, ntk=ntk: e.transpose(
                ps[3][0:E, 0:128], mtok[:, ntk, :], identsb[:, :]),
                deps=[g1, ps_user[3]] if ntk == 0 else [ps_user[3]])
            ev = sch.add("v", lambda e, ntk=ntk: e.tensor_copy(
                out=maskT[:, 128 * ntk:128 * ntk + 128], in_=ps[3][0:E, 0:128]), deps=[tp])
            ps_user[3] = ev
            mt_ev.append(ev)
        scan = sch.add("v", lambda e: e.tensor_tensor_scan(
            out=c0[:, :], data0=maskT[:, :],
            data1=onescf[0:16, 0:1].to_broadcast([16, T]), initial=0.0,
            op0=TS.add, op1=TS.mult), deps=mt_ev + [A["ones"]])
        csel = sch.add("v", lambda e: e.tensor_mul(c0[:, :], c0[:, :], maskT[:, :]),
                       deps=[scan])
        ct_ev = []
        for ntk in range(NT):
            tp = sch.add("t", lambda e, ntk=ntk: e.transpose(
                ps[3][0:128, 0:E], c0[:, 128 * ntk:128 * ntk + 128], identsb[0:16, 0:16]),
                deps=[csel, ps_user[3]] if ntk == 0 else [ps_user[3]])
            ev = sch.add("v", lambda e, ntk=ntk: e.tensor_copy(
                out=ctok[:, ntk, :], in_=ps[3][0:128, 0:E]), deps=[tp])
            ps_user[3] = ev
            ct_ev.append(ev)
        NW = 3 + 2 * E
        rbA = sch.add("q", lambda e: e.dma_start(out=ioNhl[:, 0, :], in_=iotaNhi[:, :]))
        rbB = sch.add("q", lambda e: e.dma_start(out=ioNhl[:, 1, :], in_=iotaNlo[:, :]))
        rb1 = sch.add("v", lambda e: e.tensor_copy(out=rhs35[:, :, 0:1],
                                                   in_=ioNhl[:, 0, :].unsqueeze(2)),
                      deps=[rbA])
        rb2 = sch.add("v", lambda e: e.tensor_copy(out=rhs35[:, :, 1:2],
                                                   in_=ioNhl[:, 1, :].unsqueeze(2)),
                      deps=[rbB])
        rb3 = sch.add("v", lambda e: e.memset(rhs35[:, :, 2:3], 1.0))
        ahi = sch.add("v", lambda e: e.tensor_copy(out=ahib[:, 0:NT * E],
                                                   in_=esb[:, 0:NT * E]), deps=[aff])
        rb4 = sch.add("v", lambda e: e.tensor_copy(
            out=rhs35[:, :, 3:3 + E],
            in_=ahib.rearrange("p (n e) -> p n e", e=E)[:, :, :]), deps=[ahi])
        ahf = sch.add("v", lambda e: e.tensor_copy(
            out=logit[:, :, :], in_=ahib.rearrange("p (n e) -> p n e", e=E)[:, :, :]),
            deps=[ahi, g1])
        rb5 = sch.add("v", lambda e: e.tensor_tensor(
            out=rhs35[:, :, 3 + E:3 + 2 * E],
            in0=esb.rearrange("p (n e) -> p n e", e=E)[:, :, :],
            in1=logit[:, :, :], op=TS.subtract), deps=[ahf])
        rbs = [rb1, rb2, rb3, rb4, rb5]
        prev_oh = [[], []]
        pend_tail = None
        pyb_rd = [[], []]
        odl = []
        for ei in range(EH):
            bA = 2 + 2 * (ei % 2)
            mm_last = [None] * NSL
            for ntk in range(NT):
                buf = ntk % 2
                oh = sch.add("v", lambda e, ei=ei, ntk=ntk, buf=buf: e.tensor_scalar(
                    out=ohb[:, buf, 0:CAP], in0=io256[:, 0:CAP],
                    scalar1=ctok[:, ntk, ei:ei + 1], scalar2=None, op0=TS.is_equal),
                    deps=[ct_ev[ntk], A["io256"]] + prev_oh[buf])
                prev_oh[buf] = []
                for sl in range(NSL):
                    w = min(128, CAP - 128 * sl)
                    dep0 = ([oh] + rbs + [ps_user[bA + sl]]) if ntk == 0 else [oh]
                    mm = sch.add(
                        "t", lambda e, ntk=ntk, sl=sl, w=w, buf=buf, bA=bA,
                        s=(ntk == 0), p=(ntk == NT - 1):
                        e.matmul(ps[bA + sl][0:NW, 0:w], rhs35[:, ntk, :],
                                 ohb[:, buf, 128 * sl:128 * sl + w], start=s, stop=p),
                        deps=dep0)
                    mm_last[sl] = mm
                    prev_oh[buf].append(mm)
            ev_sl = []
            for sl in range(NSL):
                w = min(128, CAP - 128 * sl)
                evc = sch.add("v", lambda e, ei=ei, sl=sl, w=w, bA=bA: e.tensor_copy(
                    out=ohsb[0:NW, ei % 2, 128 * sl:128 * sl + w],
                    in_=ps[bA + sl][0:NW, 0:w]), deps=[mm_last[sl]])
                ps_user[bA + sl] = evc
                ev_sl.append(evc)
            if pend_tail is not None:
                pend_tail()

            def pend_tail(ei=ei, ev_sl=ev_sl):
                for sl in range(NSL):
                    w = min(128, CAP - 128 * sl)
                    tp = sch.add("t", lambda e, ei=ei, sl=sl, w=w: e.transpose(
                        ps[6 + sl][0:w, 0:NW], ohsb[0:NW, ei % 2, 128 * sl:128 * sl + w],
                        identsb[0:NW, 0:NW]), deps=[ev_sl[sl], ps_user[6 + sl]])
                    evy = sch.add("v", lambda e, sl=sl, w=w: e.tensor_copy(
                        out=pyb[0:w, sl, :], in_=ps[6 + sl][0:w, 0:NW]),
                        deps=[tp] + pyb_rd[sl])
                    ps_user[6 + sl] = evy
                    f1 = sch.add("v", lambda e, sl=sl, w=w: e.tensor_scalar(
                        out=smallf[0:w, 0:1], in0=pyb[0:w, sl, 2:3], scalar1=-float(T),
                        scalar2=float(T), op0=TS.mult, op1=TS.add), deps=[evy])
                    f2 = sch.add("v", lambda e, sl=sl, w=w: e.tensor_scalar(
                        out=smallf[0:w, 1:2], in0=pyb[0:w, sl, 0:1], scalar1=32.0,
                        scalar2=None, op0=TS.mult), deps=[evy])
                    f3 = sch.add("v", lambda e, sl=sl, w=w: e.tensor_tensor(
                        out=smallf[0:w, 2:3], in0=smallf[0:w, 1:2],
                        in1=pyb[0:w, sl, 1:2], op=TS.add), deps=[f2])
                    f4 = sch.add("v", lambda e, w=w: e.tensor_tensor(
                        out=smallf[0:w, 3:4], in0=smallf[0:w, 2:3],
                        in1=smallf[0:w, 0:1], op=TS.add), deps=[f1, f3])
                    f5 = sch.add("v", lambda e, ei=ei, sl=sl, w=w: e.tensor_copy(
                        out=idx_sb[0:w, ei:ei + 1, sl:sl + 1],
                        in_=smallf[0:w, 3:4].unsqueeze(2)), deps=[f4])
                    fg = sch.add("v", lambda e, ei=ei, sl=sl, w=w: e.tensor_tensor(
                        out=gate_sb[0:w, ei:ei + 1, sl:sl + 1],
                        in0=pyb[0:w, sl, 3 + ei:4 + ei].unsqueeze(2),
                        in1=pyb[0:w, sl, 3 + E + ei:4 + E + ei].unsqueeze(2),
                        op=TS.add), deps=[evy])
                    pyb_rd[sl] = [f5, fg]
                    odl.append(sch.add("q", lambda e, sl=sl, w=w, ei=ei: e.dma_start(
                        out=idx_o[ei:ei + 1, 128 * sl:128 * sl + w].rearrange(
                            "a p -> p a").unsqueeze(2),
                        in_=idx_sb[0:w, ei:ei + 1, sl:sl + 1],
                        allow_slow_non_contiguous=True), deps=[f5]))
                    odl.append(sch.add("q", lambda e, sl=sl, w=w, ei=ei: e.dma_start(
                        out=gate_o[ei:ei + 1, 128 * sl:128 * sl + w].rearrange(
                            "a p -> p a").unsqueeze(2),
                        in_=gate_sb[0:w, ei:ei + 1, sl:sl + 1],
                        allow_slow_non_contiguous=True), deps=[fg]))
        pend_tail()
        return {"ln_dma": ln_dma, "odl": odl, "idx_sb": idx_sb,
                "gate_sb": gate_sb}


def build_l2(TTOT=16384, T=4096, D=1024, DFF=4096, CAP=256, NEX=2, NB=4):
    NSL = ceil_div(CAP, 128)
    NCELL = NEX * NB
    nc = bass.Bass("TRN2", target_bir_lowering=False, debug=False)
    io = {}
    io["xmid_all"] = nc.dram_tensor("xmid_all", [TTOT, D], BF16, kind="ExternalInput").ap()
    io["idx_i"] = nc.dram_tensor("idx_i", [NCELL, CAP], U32, kind="ExternalInput").ap()
    io["gate_i"] = nc.dram_tensor("gate_i", [NCELL, CAP], F32, kind="ExternalInput").ap()
    io["w1_i"] = nc.dram_tensor("w1_i", [NEX, D, DFF], F32, kind="ExternalInput").ap()
    io["w2_i"] = nc.dram_tensor("w2_i", [NEX, DFF, D], F32, kind="ExternalInput").ap()
    io["b1_i"] = nc.dram_tensor("b1_i", [NEX, DFF], F32, kind="ExternalInput").ap()
    io["b2_i"] = nc.dram_tensor("b2_i", [NEX, D], F32, kind="ExternalInput").ap()
    io["identb"] = nc.dram_tensor("identb", [128, 128], BF16, kind="ExternalInput").ap()
    io["eo_o"] = nc.dram_tensor("eo_o", [NCELL, NSL, 128, D], BF16, kind="ExternalOutput").ap()
    sch = Sched()
    with ExitStack() as ctx:
        sb = lambda name, shape, dt: ctx.enter_context(nc.sbuf_tensor(name, shape, dt))
        ps = mk_psum(nc, ctx)
        st = emit_l2(nc, sch, sb, ps, io, TTOT=TTOT, T=T, D=D, DFF=DFF, CAP=CAP,
                     NEX=NEX, NB=NB, base_stride=T)
        fin = sch.add("g", lambda e: e.nop(), inc=1, deps=st["all_eo"])
        sch.emit_all(nc, ctx)
    return nc


def emit_l2(nc, sch, sb, ps, io, TTOT=16384, T=4096, D=1024, DFF=4096, CAP=256,
            NEX=2, NB=4, base_stride=None, cell_base=None, chunked=None,
            predeps=()):
    DC = D // 128
    FC = DFF // 128
    NSL = ceil_div(CAP, 128)
    NCELL = NEX * NB
    if cell_base is None:
        cell_base = [b * base_stride for b in range(NB)]
    xmid_all = io["xmid_all"]
    idx_i = io.get("idx_i"); gate_i = io.get("gate_i")
    w1_i = io["w1_i"]; w2_i = io["w2_i"]; b1_i = io["b1_i"]; b2_i = io["b2_i"]
    identb = io["identb"]; eo_o = io["eo_o"]
    predeps = list(predeps)
    if True:
        w1b = sb("w1b", [128, DC, DFF], BF16)
        w2b = sb("w2b", [128, FC, D], BF16)
        b1b = sb("b1b", [128, FC], F32)
        b2b = sb("b2b", [1, NEX * D], BF16)
        idxs = sb("idxs", [128, NCELL, NSL], U32)
        idxf = sb("idxf", [128, NSL], F32)
        kbuf = sb("kbuf", [128, 2, NSL], F32)
        gts = sb("gts", [128, NCELL, NSL], F32)
        gath = sb("gath", [128, NSL, D], BF16)
        eiT = sb("eiT", [128, DC, CAP], BF16)
        hb = sb("hb", [128, FC, CAP], BF16)
        eob = sb("eob", [128, 2, 512], BF16)
        idb = sb("idb", [128, 128], BF16)
        onesr = sb("onesr", [1, 128], BF16)
        A = {}
        A["ident"] = sch.add("q", lambda e: e.dma_start(out=idb[:, :], in_=identb[:, :]),
                             deps=predeps)
        PW = min(128, CAP)
        idx_src = io.get("idx_pcs")
        if idx_src is None:
            idx_src = idx_i.rearrange("c (s p) -> p c s", p=PW)
        gate_src = io.get("gate_pcs")
        if gate_src is None:
            gate_src = gate_i.rearrange("c (s p) -> p c s", p=PW)
        idx_out = idxs[0:PW, :, :]
        gts_out = gts[0:PW, :, :]
        if len(idx_src.shape) == 4:
            idx_out = idx_out.rearrange("p (x bb) s -> p x bb s", x=NEX)
            gts_out = gts_out.rearrange("p (x bb) s -> p x bb s", x=NEX)
        A["idx"] = sch.add("q", lambda e: e.dma_start(
            out=idx_out, in_=idx_src, allow_slow_non_contiguous=True),
            deps=predeps)
        A["gts"] = sch.add("q", lambda e: e.dma_start(
            out=gts_out, in_=gate_src, allow_slow_non_contiguous=True),
            deps=predeps)
        A["onesr"] = sch.add("v", lambda e: e.memset(onesr[:, :], 1.0))

        ps_user = [None] * 8
        all_eo = []
        w_rd = [None, None]
        b_rd = [None, None]
        gath_rd = None
        eiT_rd = None
        hb_rd = None
        eob_rd = [None, None]
        idxf_rd = None

        for ex in range(NEX):
            wl1 = []
            wl2 = []
            for c in range(DC):
                wl1.append(sch.add("g", lambda e, ex=ex, c=c: e.dma_start(
                    out=w1b[:, c, :], in_=w1_i[ex, 128 * c:128 * c + 128, :]),
                    deps=(predeps + [w_rd[0]]) if c == 0 else []))
            for f in range(FC):
                wl2.append(sch.add("g", lambda e, ex=ex, f=f: e.dma_start(
                    out=w2b[:, f, :], in_=w2_i[ex, 128 * f:128 * f + 128, :]),
                    deps=[w_rd[1]] if f == 0 and w_rd[1] else []))
            bl = sch.add("q", lambda e, ex=ex: e.dma_start(
                out=b1b[:, :], in_=b1_i[ex:ex + 1, :].rearrange("a (t p) -> p (a t)", p=128),
                allow_slow_non_contiguous=True),
                deps=[b_rd[0]] if b_rd[0] else [])
            b2l = sch.add("g", lambda e, ex=ex: e.dma_start(
                out=b2b[:, ex * D:(ex + 1) * D], in_=b2_i[ex:ex + 1, :]))
            for b in range(NB):
                cell = ex * NB + b
                i1 = sch.add("v", lambda e, cell=cell: e.tensor_copy(
                    out=idxf[0:PW, :], in_=idxs[0:PW, cell, :]),
                    deps=[A["idx"]] + ([idxf_rd] if idxf_rd else []))
                if chunked is not None and chunked[1] > 1:
                    RPC, NCH = chunked
                    prevk = i1
                    for j in range(1, NCH):
                        op0 = sch.add("v", lambda e, j=j: e.tensor_scalar(
                            out=kbuf[0:PW, 1, :], in0=idxf[0:PW, :],
                            scalar1=float(j * RPC) - 0.5, scalar2=None,
                            op0=TS.is_gt), deps=[prevk])
                        if j == 1:
                            prevk = sch.add("v", lambda e: e.tensor_copy(
                                out=kbuf[0:PW, 0, :], in_=kbuf[0:PW, 1, :]),
                                deps=[op0])
                        else:
                            prevk = sch.add("v", lambda e: e.tensor_add(
                                kbuf[0:PW, 0, :], kbuf[0:PW, 0, :], kbuf[0:PW, 1, :]),
                                deps=[op0, prevk])
                    ksc = sch.add("v", lambda e: e.tensor_scalar(
                        out=kbuf[0:PW, 0, :], in0=kbuf[0:PW, 0, :],
                        scalar1=float((NCORES - 1) * RPC), scalar2=None,
                        op0=TS.mult), deps=[prevk])
                    i2 = sch.add("v", lambda e, b=b: e.tensor_scalar(
                        out=idxf[0:PW, :], in0=idxf[0:PW, :],
                        scalar1=float(cell_base[b]), scalar2=None, op0=TS.add),
                        deps=[i1, ksc])
                    i2 = sch.add("v", lambda e: e.tensor_add(
                        idxf[0:PW, :], idxf[0:PW, :], kbuf[0:PW, 0, :]),
                        deps=[i2])
                else:
                    i2 = sch.add("v", lambda e, b=b: e.tensor_scalar(
                        out=idxf[0:PW, :], in0=idxf[0:PW, :],
                        scalar1=float(cell_base[b]), scalar2=None, op0=TS.add),
                        deps=[i1])
                i3 = sch.add("v", lambda e, cell=cell: e.tensor_copy(
                    out=idxs[0:PW, cell, :], in_=idxf[0:PW, :]), deps=[i2])
                g_ops = []
                for sl in range(NSL):
                    wj = min(128, CAP - 128 * sl)
                    g_ops.append(sch.add("g", lambda e, cell=cell, sl=sl, wj=wj: e.indirect_dma_start(
                        out=gath[0:wj, sl, :], out_offset=None, in_=xmid_all[:, :],
                        in_offset=bass.IndirectOffsetOnAxis(idxs[0:wj, cell, sl:sl + 1], 0)),
                        deps=[i3] + ([gath_rd] if gath_rd else []), ind=True))
                tp_ev = []
                for sl in range(NSL):
                    wj = min(128, CAP - 128 * sl)
                    for c in range(DC):
                        tp = sch.add("t", lambda e, sl=sl, c=c, wj=wj: e.transpose(
                            ps[0][0:128, 0:128].bitcast(BF16)[:, 0:wj], gath[0:wj, sl, 128 * c:128 * c + 128],
                            idb[0:wj, 0:wj]), deps=[g_ops[sl], A["ident"], ps_user[0]])
                        ev = sch.add("v", lambda e, sl=sl, c=c, wj=wj: e.tensor_copy(
                            out=eiT[:, c, 128 * sl:128 * sl + wj],
                            in_=ps[0][0:128, 0:128].bitcast(BF16)[:, 0:wj]),
                            deps=[tp] + ([eiT_rd] if eiT_rd and sl == 0 and c == 0 else []))
                        ps_user[0] = ev
                        tp_ev.append(ev)
                h_ev = []
                for ft in range(FC):
                    pb = 1 + ft % 2
                    acc = None
                    for c in range(DC):
                        acc = sch.add("t", lambda e, ft=ft, c=c, pb=pb, s=(c == 0), p=(c == DC - 1):
                                      e.matmul(ps[pb][0:128, 0:CAP], w1b[:, c, 128 * ft:128 * ft + 128],
                                               eiT[:, c, :], start=s, stop=p),
                                      deps=(tp_ev + [wl1[0], ps_user[pb]]) if c == 0
                                      else [wl1[c]])
                    hg = sch.add("s", lambda e, ft=ft, pb=pb: e.activation(
                        out=hb[:, ft, :], in_=ps[pb][0:128, 0:CAP], func=AF.Gelu_apprx_tanh,
                        bias=b1b[:, ft:ft + 1], scale=1.0),
                        deps=[acc, bl] + ([hb_rd] if hb_rd and ft == 0 else []))
                    ps_user[pb] = hg
                    h_ev.append(hg)
                eo_dma = []
                for js in range(NSL):
                    wj = min(128, CAP - 128 * js)
                    for dh in range(ceil_div(D, 512)):
                        pb = 3 + (2 * js + dh) % 2
                        acc = None
                        for ft in range(FC):
                            wd = min(512, D - 512 * dh)
                            acc = sch.add("t", lambda e, js=js, dh=dh, ft=ft, pb=pb, wj=wj, wd=wd,
                                          s=(ft == 0):
                                          e.matmul(ps[pb][0:wj, 0:wd],
                                                   hb[:, ft, 128 * js:128 * js + wj],
                                                   w2b[:, ft, 512 * dh:512 * dh + wd],
                                                   start=s, stop=False),
                                          deps=(h_ev + [wl2[0], ps_user[pb]]) if ft == 0
                                          else [wl2[ft]])
                        wd = min(512, D - 512 * dh)
                        acc = sch.add("t", lambda e, ex=ex, dh=dh, pb=pb, wj=wj, wd=wd: e.matmul(
                            ps[pb][0:wj, 0:wd], onesr[:, 0:wj],
                            b2b[:, ex * D + 512 * dh:ex * D + 512 * dh + wd],
                            start=False, stop=True),
                            deps=[A["onesr"], b2l])
                        gm = sch.add("v", lambda e, cell=cell, js=js, dh=dh, pb=pb, wj=wj, wd=wd: e.tensor_scalar(
                            out=eob[0:wj, dh % 2, 0:wd], in0=ps[pb][0:wj, 0:wd],
                            scalar1=gts[0:wj, cell, js:js + 1], scalar2=None, op0=TS.mult),
                            deps=[acc, A["gts"]] + ([eob_rd[dh % 2]] if eob_rd[dh % 2] else []))
                        ps_user[pb] = gm
                        dma = sch.add("q", lambda e, cell=cell, js=js, dh=dh, wj=wj, wd=wd: e.dma_start(
                            out=eo_o[cell, js, 0:wj, 512 * dh:512 * dh + wd], in_=eob[0:wj, dh % 2, 0:wd]),
                            deps=[gm])
                        eob_rd[dh % 2] = dma
                        eo_dma.append(dma)
                        all_eo.append(dma)
                gath_rd = tp_ev[-1]
                eiT_rd = h_ev[-1] if h_ev else None
                hb_rd = eo_dma[-1]
                idxf_rd = i3
            w_rd = [acc, acc]
            b_rd = [bl, b2l]
        return {"all_eo": all_eo}


def build_l3(T=4096, D=1024, CAP=256, E=16):
    NSL = ceil_div(CAP, 128)
    nc = bass.Bass("TRN2", target_bir_lowering=False, debug=False)
    io = {}
    io["eo_i"] = nc.dram_tensor("eo_i", [E, NSL, 128, D], BF16, kind="ExternalInput").ap()
    io["idx_i"] = nc.dram_tensor("idx_i", [E, CAP], U32, kind="ExternalInput").ap()
    io["xmid_i"] = nc.dram_tensor("xmid_i", [T, D], F32, kind="ExternalInput").ap()
    io["out_o"] = nc.dram_tensor("out_o", [T, D], F32, kind="ExternalOutput").ap()
    io["accum"] = [nc.dram_tensor("accum0", [T + 128, D], F32).ap(),
                   nc.dram_tensor("accum1", [T + 128, D], F32).ap()]
    sch = Sched()
    with ExitStack() as ctx:
        sb = lambda name, shape, dt: ctx.enter_context(nc.sbuf_tensor(name, shape, dt))
        st = emit_l3(nc, sch, sb, io, T=T, D=D, CAP=CAP, E=E)
        fin = sch.add("g", lambda e: e.nop(), inc=1, deps=st["out_dma"])
        sch.emit_all(nc, ctx)
    return nc


def emit_l3(nc, sch, sb, io, T=4096, D=1024, CAP=256, E=16, predeps=(),
            idx_sb_ext=None):
    NSL = ceil_div(CAP, 128)
    NT = T // 128
    eo_i = io.get("eo_i"); xmid_i = io["xmid_i"]; out_o = io["out_o"]; accum = io["accum"]
    predeps = list(predeps)
    if True:
        NBUF = 6
        zt = sb("zt", [128, D], F32)
        idxs = sb("idxs", [128, E, NSL], U32)
        eosb = sb("eosb", [128, NBUF, D], BF16)
        acc_sb = sb("acc_sb", [128, 2, 2, D], F32)
        sqs = sb("sqs", [128, 2, D], F32)
        stats = sb("stats", [128, 16], F32)
        orow = sb("orow", [128, 2, D], F32)

        A = {}
        A["z"] = sch.add("v", lambda e: e.memset(zt[:, :], 0.0), deps=predeps)
        PW = min(128, CAP)
        if idx_sb_ext is None:
            idx_i = io["idx_i"]
            A["idx"] = sch.add("q", lambda e: e.dma_start(
                out=idxs[0:PW, :, :],
                in_=idx_i.rearrange("c (s p) -> p c s", p=PW),
                allow_slow_non_contiguous=True), deps=predeps)
        else:
            idxs = idx_sb_ext
            A["idx"] = None
        xm_deps = io.get("xmid_wr") or []
        zd = [[], []]
        for nt in range(NT):
            zd[0].append(sch.add("q", lambda e, nt=nt: e.dma_start(
                out=accum[0][128 * nt:128 * nt + 128, :],
                in_=xmid_i[128 * nt:128 * nt + 128, :]),
                deps=([xm_deps[nt]] if xm_deps else predeps)))
        zd[0].append(sch.add("q", lambda e: e.dma_start(
            out=accum[0][128 * NT:128 * NT + 128, :], in_=zt[:, :]), deps=[A["z"]]))
        for nt in range(NT + 1):
            zd[1].append(sch.add("q", lambda e, nt=nt: e.dma_start(
                out=accum[1][128 * nt:128 * nt + 128, :], in_=zt[:, :]),
                deps=[A["z"]]))
        eo_rows = io.get("eo_rows")
        eosel_sb = io.get("eosel_sb")
        last_sc = [None, None]
        eol = [None] * NBUF
        for ei in range(E):
            for sl in range(NSL):
                wj = min(128, CAP - 128 * sl)
                k = ei * NSL + sl
                par = k % 2
                buf = k % NBUF
                if eo_rows is None:
                    ld = sch.add("q", lambda e, ei=ei, sl=sl, buf=buf, wj=wj: e.dma_start(
                        out=eosb[0:wj, buf, :], in_=eo_i[ei, sl, 0:wj, :]),
                        deps=[eol[buf]] if eol[buf] else [])
                else:
                    ld = sch.add("g", lambda e, ei=ei, sl=sl, buf=buf, wj=wj: e.indirect_dma_start(
                        out=eosb[0:wj, buf, :], out_offset=None,
                        in_=eo_rows,
                        in_offset=bass.IndirectOffsetOnAxis(eosel_sb[0:wj, ei, sl:sl + 1], 0)),
                        deps=predeps + ([eol[buf]] if eol[buf] else []))
                sc = sch.add("g", lambda e, ei=ei, sl=sl, par=par, buf=buf, wj=wj:
                             e.indirect_dma_start(
                    out=accum[par][:, :], out_offset=bass.IndirectOffsetOnAxis(
                        idxs[0:wj, ei, sl:sl + 1], 0),
                    in_=eosb[0:wj, buf, :], in_offset=None,
                    compute_op=TS.add),
                    deps=[ld, A["idx"]] + zd[par], ind=1 + par)
                eol[buf] = sc
                last_sc[par] = sc
        out_dma = []
        stats_rd = [None, None]
        sq_rd = [None, None]
        buf_rd = [None, None]
        for nt in range(NT):
            bb = nt % 2
            l1 = sch.add("q", lambda e, nt=nt, bb=bb: e.dma_start(
                out=acc_sb[:, bb, 0, :], in_=accum[0][128 * nt:128 * nt + 128, :]),
                deps=[last_sc[0]] + ([buf_rd[bb]] if buf_rd[bb] else []))
            l1b = sch.add("q", lambda e, nt=nt, bb=bb: e.dma_start(
                out=acc_sb[:, bb, 1, :], in_=accum[1][128 * nt:128 * nt + 128, :]),
                deps=[last_sc[1]] + ([buf_rd[bb]] if buf_rd[bb] else []))
            r1 = sch.add("v", lambda e, bb=bb: e.tensor_add(
                orow[:, bb, :], acc_sb[:, bb, 0, :], acc_sb[:, bb, 1, :]),
                deps=[l1, l1b] + ([buf_rd[bb]] if buf_rd[bb] else []))
            so = 8 * bb
            c1 = sch.add("v", lambda e, bb=bb, so=so: e.reduce_sum(
                stats[:, so + 0:so + 1], orow[:, bb, :], axis=AX.X),
                deps=[r1] + ([stats_rd[bb]] if stats_rd[bb] else []))
            c2 = sch.add("v", lambda e, so=so: e.tensor_scalar(
                out=stats[:, so + 0:so + 1], in0=stats[:, so + 0:so + 1],
                scalar1=1.0 / D, scalar2=None, op0=TS.mult), deps=[c1])
            sqv = sch.add("s", lambda e, bb=bb: e.activation(
                out=sqs[:, bb, :], in_=orow[:, bb, :], func=AF.Square),
                deps=[r1] + ([sq_rd[bb]] if sq_rd[bb] else []))
            c4 = sch.add("v", lambda e, bb=bb, so=so: e.reduce_sum(
                stats[:, so + 1:so + 2], sqs[:, bb, :], axis=AX.X), deps=[sqv])
            c5 = sch.add("v", lambda e, so=so: e.tensor_scalar(
                out=stats[:, so + 1:so + 2], in0=stats[:, so + 1:so + 2],
                scalar1=1.0 / D, scalar2=None, op0=TS.mult), deps=[c4])
            c6 = sch.add("v", lambda e, so=so: e.tensor_mul(
                stats[:, so + 2:so + 3], stats[:, so + 0:so + 1],
                stats[:, so + 0:so + 1]), deps=[c2])
            c7 = sch.add("v", lambda e, so=so: e.tensor_tensor(
                out=stats[:, so + 1:so + 2], in0=stats[:, so + 1:so + 2],
                in1=stats[:, so + 2:so + 3], op=TS.subtract), deps=[c5, c6])
            c8 = sch.add("v", lambda e, so=so: e.tensor_scalar(
                out=stats[:, so + 1:so + 2], in0=stats[:, so + 1:so + 2],
                scalar1=1e-5, scalar2=None, op0=TS.add), deps=[c7])
            c9 = sch.add("s", lambda e, so=so: e.activation(
                out=stats[:, so + 3:so + 4], in_=stats[:, so + 1:so + 2],
                func=AF.Sqrt), deps=[c8])
            c10 = sch.add("v", lambda e, so=so: e.reciprocal(
                stats[:, so + 4:so + 5], stats[:, so + 3:so + 4]), deps=[c9])
            xm = sch.add("v", lambda e, bb=bb, so=so: e.tensor_scalar(
                out=orow[:, bb, :], in0=orow[:, bb, :], scalar1=stats[:, so + 0:so + 1],
                scalar2=stats[:, so + 4:so + 5], op0=TS.subtract, op1=TS.mult),
                deps=[c10])
            sq_rd[bb] = c4
            d1 = sch.add("q", lambda e, nt=nt, bb=bb: e.dma_start(
                out=out_o[128 * nt:128 * nt + 128, :], in_=orow[:, bb, :]), deps=[xm])
            out_dma.append(d1)
            buf_rd[bb] = d1
            stats_rd[bb] = xm
        return {"out_dma": out_dma}


_T, _D, _H, _E, _CAP, _DFF, _B = 4096, 1024, 16, 16, 256, 4096, 4


def _consts(T, D):
    import ml_dtypes
    NT = T // 128
    masks = np.zeros((128, 4, 512), np.float32)
    ar = np.arange(512)
    for r in range(4):
        for p in range(128):
            masks[p, r, :] = (128 * r + p <= ar)
    return {
        "identf": np.eye(128, dtype=np.float32),
        "masks": masks.astype(ml_dtypes.bfloat16),
        "iota256": np.broadcast_to(np.arange(256, dtype=np.float32) + 1.0,
                                   (128, 256)).copy(),
        "iotaN": (128 * np.arange(NT, dtype=np.float32)[None, :]
                  + np.arange(128, dtype=np.float32)[:, None]).copy(),
        "iotaNhi": np.asarray((128 * np.arange(NT)[None, :]
                   + np.arange(128)[:, None]) // 32, dtype=ml_dtypes.bfloat16),
        "iotaNlo": np.asarray((128 * np.arange(NT)[None, :]
                   + np.arange(128)[:, None]) % 32, dtype=ml_dtypes.bfloat16),
        "ones_in": np.ones((128, 128), np.float32),
    }


def kernel(**inputs):
    return kernel_fused(**inputs)


def kernel_v0_multilaunch(L0_attn_x, L0_attn_W_qkv, L0_attn_W_proj, L0_ffn_gate_query,
           L0_ffn_log_temp, L0_ffn_w1, L0_ffn_b1, L0_ffn_w2, L0_ffn_b2):
    import ml_dtypes
    T, D, H, E, CAP, DFF, B = _T, _D, _H, _E, _CAP, _DFF, _B
    x = np.ascontiguousarray(np.asarray(L0_attn_x, dtype=np.float32))
    wqkv = np.asarray(L0_attn_W_qkv, dtype=np.float32)
    wproj = np.ascontiguousarray(np.asarray(L0_attn_W_proj, dtype=np.float32))
    gq = np.asarray(L0_ffn_gate_query, dtype=np.float32)
    lt = np.asarray(L0_ffn_log_temp, dtype=np.float32).reshape(1, 1)
    w1 = np.asarray(L0_ffn_w1, dtype=np.float32)
    b1 = np.asarray(L0_ffn_b1, dtype=np.float32)
    w2 = np.asarray(L0_ffn_w2, dtype=np.float32)
    b2 = np.asarray(L0_ffn_b2, dtype=np.float32)

    HP = H // 2
    wq_p = np.empty((HP, D, 384), np.float32)
    for hp in range(HP):
        wq_p[hp, :, 0:128] = wqkv[:, 128 * hp:128 * hp + 128]
        wq_p[hp, :, 128:256] = wqkv[:, D + 128 * hp:D + 128 * hp + 128]
        wq_p[hp, :, 256:384] = wqkv[:, 2 * D + 128 * hp:2 * D + 128 * hp + 128]
    cst = _consts(T, D)
    gqT = np.ascontiguousarray(gq.T)

    nc1 = build_l1(T=T, D=D, H=H, E=E, CAP=CAP)
    in1 = []
    for c in range(NCORES):
        b = c // 2
        in1.append(dict(xT=np.ascontiguousarray(x[b].T), wqkv_p=wq_p, wproj=wproj,
                        gqT=gqT, ltemp=lt, **cst))
    r1 = run_bass_kernel_spmd(nc1, in1, core_ids=list(range(NCORES))).results

    xmid_all = np.concatenate([np.asarray(r1[2 * b]["xmid_b"]) for b in range(B)],
                              axis=0)
    identb = np.eye(128, dtype=ml_dtypes.bfloat16)
    nc2 = build_l2(TTOT=B * T, T=T, D=D, DFF=DFF, CAP=CAP, NEX=2, NB=B)
    in2 = []
    for c in range(NCORES):
        exs = [2 * c, 2 * c + 1]
        idx_i = np.empty((2 * B, CAP), np.uint32)
        gate_i = np.empty((2 * B, CAP), np.float32)
        for xi, ex in enumerate(exs):
            for b in range(B):
                idx_i[xi * B + b] = np.asarray(r1[2 * b]["idx_o"])[ex]
                gate_i[xi * B + b] = np.asarray(r1[2 * b]["gate_o"])[ex]
        in2.append(dict(xmid_all=xmid_all, idx_i=idx_i, gate_i=gate_i,
                        w1_i=np.ascontiguousarray(w1[exs]),
                        w2_i=np.ascontiguousarray(w2[exs]),
                        b1_i=np.ascontiguousarray(b1[exs]),
                        b2_i=np.ascontiguousarray(b2[exs]), identb=identb))
    r2 = run_bass_kernel_spmd(nc2, in2, core_ids=list(range(NCORES))).results

    NSL = ceil_div(CAP, 128)
    nc3 = build_l3(T=T, D=D, CAP=CAP, E=E)
    in3 = []
    for c in range(NCORES):
        b = c // 2
        eo_i = np.empty((E, NSL, 128, D), ml_dtypes.bfloat16)
        for e in range(E):
            eo_i[e] = np.asarray(r2[e // 2]["eo_o"]).reshape(2 * B, NSL, 128, D)[
                (e % 2) * B + b]
        in3.append(dict(eo_i=eo_i, idx_i=np.asarray(r1[2 * b]["idx_o"]),
                        xmid_i=np.asarray(r1[2 * b]["xmid_f"])))
    r3 = run_bass_kernel_spmd(nc3, in3, core_ids=list(range(NCORES))).results

    out = np.stack([np.asarray(r3[2 * b]["out_o"]) for b in range(B)], axis=0)
    return out.astype(np.float32)


def build_fused(T=4096, D=1024, H=16, E=16, CAP=256, DFF=4096, NB=4,
                BIS_ITERS=36, SINK_ITERS=6):
    HP = H // 2
    NT = T // 128
    NSL = ceil_div(CAP, 128)
    NEX = E // NCORES
    NCELL = NEX * NB
    PW = min(128, CAP)
    nc = bass.Bass("TRN2", target_bir_lowering=False, debug=False)
    io1 = {}
    io1["xT"] = nc.dram_tensor("xT", [D, T], F32, kind="ExternalInput").ap()
    HPL = HP // 2
    NJ = T // 512
    io1["wqkv_p"] = nc.dram_tensor("wqkv_p", [HPL, D, 3 * 128], F32, kind="ExternalInput").ap()
    io1["wproj"] = nc.dram_tensor("wproj", [D, D], F32, kind="ExternalInput").ap()
    io1["gqT"] = nc.dram_tensor("gqT", [D, E], F32, kind="ExternalInput").ap()
    io1["ltemp"] = nc.dram_tensor("ltemp", [1, 1], F32, kind="ExternalInput").ap()
    io1["identf"] = nc.dram_tensor("identf", [128, 128], F32, kind="ExternalInput").ap()
    io1["masks"] = nc.dram_tensor("masks", [128, 4, 512], BF16, kind="ExternalInput").ap()
    io1["iota256"] = nc.dram_tensor("iota256", [128, 256], F32, kind="ExternalInput").ap()
    io1["iotaN"] = nc.dram_tensor("iotaN", [128, NT], F32, kind="ExternalInput").ap()
    io1["iotaNhi"] = nc.dram_tensor("iotaNhi", [128, NT], BF16, kind="ExternalInput").ap()
    io1["iotaNlo"] = nc.dram_tensor("iotaNlo", [128, NT], BF16, kind="ExternalInput").ap()
    io1["ones_in"] = nc.dram_tensor("ones_in", [128, 128], F32, kind="ExternalInput").ap()
    identb = nc.dram_tensor("identb", [128, 128], BF16, kind="ExternalInput").ap()
    w1_i = nc.dram_tensor("w1_i", [NEX, D, DFF], F32, kind="ExternalInput").ap()
    w2_i = nc.dram_tensor("w2_i", [NEX, DFF, D], F32, kind="ExternalInput").ap()
    b1_i = nc.dram_tensor("b1_i", [NEX, DFF], F32, kind="ExternalInput").ap()
    b2_i = nc.dram_tensor("b2_i", [NEX, D], F32, kind="ExternalInput").ap()
    out_o = nc.dram_tensor("out_o", [T, D], F32, kind="ExternalOutput").ap()
    oacc_t = [nc.dram_tensor(f"oacc{l}_h", [128, NJ, 512], BF16) for l in range(HPL)]
    io1["oacc_h"] = [t.ap() for t in oacc_t]
    oall_t = [nc.dram_tensor(f"oall{l}_h", [NCORES, 128, NJ, 512], BF16)
              for l in range(HPL)]
    oall_rows = [t.ap().rearrange("s p j f -> (s p j) f") for t in oall_t]
    io1["osel"] = nc.dram_tensor("osel", [128, D // 128, NJ], U32,
                                 kind="ExternalInput").ap()
    xmid_f_h = nc.dram_tensor("xmid_f_h", [T, D], F32)
    xmid_b_h = nc.dram_tensor("xmid_b_h", [T, D], BF16)
    xmid_all_h = nc.dram_tensor("xmid_all_h", [NCORES * T, D], BF16)
    rall_send = nc.dram_tensor("rall_send", [E // 2, CAP, 2], U32)
    rrall_h = nc.dram_tensor("rrall_h", [NCORES, E // 2, CAP, 2], U32)
    rg_h = nc.dram_tensor("rg_h", [NCELL, CAP, 2], U32)
    eoloc_h = nc.dram_tensor("eoloc_h", [NCELL, NSL, 128, D], BF16)
    eoall_h = nc.dram_tensor("eoall_h", [NCORES, NCELL, NSL, 128, D], BF16)
    rsel = nc.dram_tensor("rsel", [NCELL, 1], U32, kind="ExternalInput")
    p3rsel = nc.dram_tensor("p3rsel", [E, 1], U32, kind="ExternalInput")
    rg3_h = nc.dram_tensor("rg3_h", [E, CAP, 2], U32)
    eosel = nc.dram_tensor("eosel", [128, E, NSL], U32, kind="ExternalInput")
    accum_h = [nc.dram_tensor("accum0_h", [T + 128, D], F32),
               nc.dram_tensor("accum1_h", [T + 128, D], F32)]
    io1["xmid_f"] = xmid_f_h.ap()
    io1["xmid_b"] = xmid_b_h.ap()

    sch = Sched()
    with ExitStack() as ctx:
        persist = {}

        def psb(name, shape, dt):
            persist[name] = ctx.enter_context(nc.sbuf_tensor(name, shape, dt))
            return persist[name]

        arena = Arena(nc, ctx, 191 * 1024)
        PERSIST = {"idx_sb", "gate_sb"}

        def mk_sb():
            def sbf(name, shape, dt):
                if name in PERSIST:
                    return psb(name, shape, dt)
                return arena.alloc(shape, dt)
            return sbf

        ps = mk_psum(nc, ctx)

        io1["idx_o"] = nc.dram_tensor("idx_oh", [E, CAP], U32).ap()
        io1["gate_o"] = nc.dram_tensor("gate_oh", [E, CAP], F32).ap()

        sch.scope = "p1_attn"

        def oag(l, deps):
            sv = sch.scope
            sch.scope = "cc_oexch"
            op = sch.add("g", lambda e, l=l: e.collective_compute(
                "AllGather", TS.bypass, replica_groups=[list(range(NCORES))],
                ins=[oacc_t[l].ap().opt()], outs=[oall_t[l].ap().opt()]),
                deps=deps, ind=True, inc=1)
            sch.scope = sv
            return op

        arena.push()
        EH = E // 2
        st1 = emit_l1(nc, sch, mk_sb(), ps, io1, T=T, D=D, H=H, E=E, CAP=CAP,
                      BIS_ITERS=BIS_ITERS, SINK_ITERS=SINK_ITERS,
                      HPL=HPL, oag=oag, oall_rows=oall_rows, sel_experts=EH)
        arena.pop()
        idx_sb = persist["idx_sb"]
        gate_sb = persist["gate_sb"]

        sch.scope = "cc_route"
        pk = []
        pk.append(sch.add("q", lambda e: e.dma_start(
            out=rall_send.ap()[:, :, 0:1].rearrange("c (s p) a -> p c s a", p=PW),
            in_=idx_sb[0:PW, 0:EH, :].unsqueeze(3),
            allow_slow_non_contiguous=True), deps=st1["odl"]))
        pk.append(sch.add("q", lambda e: e.dma_start(
            out=rall_send.ap()[:, :, 1:2].rearrange("c (s p) a -> p c s a", p=PW),
            in_=gate_sb[0:PW, 0:EH, :].unsqueeze(3).bitcast(U32),
            allow_slow_non_contiguous=True), deps=st1["odl"]))
        NCH = 1
        TPC = NT // NCH
        RPC = T // NCH
        cc2l = []
        for k in range(NCH):
            cc2l.append(sch.add("g", lambda e, k=k: e.collective_compute(
                "AllGather", TS.bypass, replica_groups=[list(range(NCORES))],
                ins=[xmid_b_h.ap()[RPC * k:RPC * (k + 1), :].opt()],
                outs=[xmid_all_h.ap()
                      [NCORES * RPC * k:NCORES * RPC * (k + 1), :].opt()]),
                deps=[st1["ln_dma"][2 * i + 1] for i in range(TPC * k, TPC * (k + 1))],
                ind=True, inc=1))
        cc2 = cc2l[-1]
        cc1 = sch.add("g", lambda e: e.collective_compute(
            "AllGather", TS.bypass, replica_groups=[list(range(NCORES))],
            ins=[rall_send.ap().opt()], outs=[rrall_h.ap().opt()]),
            deps=pk, ind=True, inc=1)
        rsel_sb = psb("rsel_sb", [NCELL, 1], U32)
        rg_sb = psb("rg_sb", [NCELL, CAP * 2], U32)
        ld_rsel = sch.add("q", lambda e: e.dma_start(out=rsel_sb[:, :], in_=rsel.ap()))
        g_rg = sch.add("g", lambda e: e.indirect_dma_start(
            out=rg_sb[:, :], out_offset=None,
            in_=rrall_h.ap().rearrange("s c w f -> (s c) (w f)"),
            in_offset=bass.IndirectOffsetOnAxis(rsel_sb[:, 0:1], 0)),
            deps=[ld_rsel, cc1], ind=True)
        st_rg = sch.add("q", lambda e: e.dma_start(out=rg_h.ap().rearrange("c w f -> c (w f)"),
                                                   in_=rg_sb[:, :]), deps=[g_rg])
        p3rsel_sb = psb("p3rsel_sb", [E, 1], U32)
        rg3_sb = psb("rg3_sb", [E, CAP * 2], U32)
        ld_p3r = sch.add("q", lambda e: e.dma_start(out=p3rsel_sb[:, :],
                                                    in_=p3rsel.ap()))
        g_rg3 = sch.add("g", lambda e: e.indirect_dma_start(
            out=rg3_sb[:, :], out_offset=None,
            in_=rrall_h.ap().rearrange("s c w f -> (s c) (w f)"),
            in_offset=bass.IndirectOffsetOnAxis(p3rsel_sb[:, 0:1], 0)),
            deps=[ld_p3r, cc1])
        st_rg3 = sch.add("q", lambda e: e.dma_start(
            out=rg3_h.ap().rearrange("c w f -> c (w f)"), in_=rg3_sb[:, :]),
            deps=[g_rg3])

        sch.scope = "p2_ffn"
        bar1 = sch.barrier()
        idx_pcs = rg_h.ap().rearrange("c (s p) f -> p c s f", p=PW)[:, :, :, 0]
        gate_pcs = rg_h.ap().rearrange("c (s p) f -> p c s f", p=PW)[:, :, :, 1].bitcast(F32)
        io2 = {"xmid_all": xmid_all_h.ap(), "idx_pcs": idx_pcs,
               "gate_pcs": gate_pcs, "w1_i": w1_i, "w2_i": w2_i,
               "b1_i": b1_i, "b2_i": b2_i, "identb": identb,
               "eo_o": eoloc_h.ap()}
        arena.push()
        st2 = emit_l2(nc, sch, mk_sb(), ps, io2, TTOT=NCORES * T, T=T, D=D,
                      DFF=DFF, CAP=CAP, NEX=NEX, NB=NB,
                      cell_base=[2 * b * (T // NCH) for b in range(NB)],
                      chunked=(T // NCH, NCH),
                      predeps=bar1 + [st_rg, cc2])
        arena.pop()

        sch.scope = "cc_eo"
        cc3 = sch.add("g", lambda e: e.collective_compute(
            "AllGather", TS.bypass, replica_groups=[list(range(NCORES))],
            ins=[eoloc_h.ap().opt()], outs=[eoall_h.ap().opt()]),
            deps=st2["all_eo"], ind=True, inc=1)
        eosel_sb = psb("eosel_sb", [128, E, NSL], U32)
        ld_eosel = sch.add("q", lambda e: e.dma_start(out=eosel_sb[:, :, :],
                                                      in_=eosel.ap()))

        sch.scope = "p3_scatter"
        bar2 = sch.barrier()
        io3 = {"eo_i": None,
               "eo_rows": eoall_h.ap().rearrange("s c n p d -> (s c n p) d"),
               "eosel_sb": eosel_sb, "idx_i": rg3_h.ap()[:, :, 0],
               "xmid_i": xmid_f_h.ap(), "out_o": out_o,
               "accum": [a.ap() for a in accum_h],
               "xmid_wr": [st1["ln_dma"][2 * i] for i in range(NT)]}
        arena.push()
        st3 = emit_l3(nc, sch, mk_sb(), io3, T=T, D=D, CAP=CAP, E=E,
                      predeps=bar2 + [cc3, ld_eosel, st_rg3], idx_sb_ext=None)
        arena.pop()
        fin = sch.add("g", lambda e: e.nop(), inc=1, deps=st3["out_dma"])
        sch.emit_all(nc, ctx)
    return nc


def _osel_table(c, T, D):
    DC = D // 128
    HPL = DC // 2
    NJ = T // 512
    b = c // 2
    osel = np.zeros((128, DC, NJ), np.uint32)
    p = np.arange(128, dtype=np.uint32)
    for cc in range(DC):
        src = 2 * b + cc // HPL
        for j in range(NJ):
            osel[:, cc, j] = (src * 128 + p) * NJ + j
    return osel


def _sel_tables(c, T, E, CAP, NB):
    NEX = E // NCORES
    NCELL = NEX * NB
    NSL = ceil_div(CAP, 128)
    bc = c // 2
    EH = E // 2
    rsel = np.zeros((NCELL, 1), np.uint32)
    for xi in range(NEX):
        for b in range(NB):
            x = NEX * c + xi
            rsel[xi * NB + b, 0] = (2 * b + x // EH) * EH + (x % EH)
    eosel = np.zeros((128, E, NSL), np.uint32)
    for e in range(E):
        snd = e // NEX
        cell = (e % NEX) * NB + bc
        for sl in range(NSL):
            base = ((snd * NCELL + cell) * NSL + sl) * 128
            eosel[:, e, sl] = base + np.arange(128)
    p3sel = np.zeros((E, 1), np.uint32)
    for e in range(E):
        p3sel[e, 0] = (2 * bc + e // EH) * EH + (e % EH)
    return rsel, eosel, p3sel


def kernel_fused(L0_attn_x, L0_attn_W_qkv, L0_attn_W_proj, L0_ffn_gate_query,
                 L0_ffn_log_temp, L0_ffn_w1, L0_ffn_b1, L0_ffn_w2, L0_ffn_b2):
    import ml_dtypes
    T, D, H, E, CAP, DFF, B = _T, _D, _H, _E, _CAP, _DFF, _B
    x = np.ascontiguousarray(np.asarray(L0_attn_x, dtype=np.float32))
    wqkv = np.asarray(L0_attn_W_qkv, dtype=np.float32)
    wproj = np.ascontiguousarray(np.asarray(L0_attn_W_proj, dtype=np.float32))
    gq = np.asarray(L0_ffn_gate_query, dtype=np.float32)
    lt = np.asarray(L0_ffn_log_temp, dtype=np.float32).reshape(1, 1)
    w1 = np.asarray(L0_ffn_w1, dtype=np.float32)
    b1 = np.asarray(L0_ffn_b1, dtype=np.float32)
    w2 = np.asarray(L0_ffn_w2, dtype=np.float32)
    b2 = np.asarray(L0_ffn_b2, dtype=np.float32)
    HP = H // 2
    wq_p = np.empty((HP, D, 384), np.float32)
    for hp in range(HP):
        wq_p[hp, :, 0:128] = wqkv[:, 128 * hp:128 * hp + 128]
        wq_p[hp, :, 128:256] = wqkv[:, D + 128 * hp:D + 128 * hp + 128]
        wq_p[hp, :, 256:384] = wqkv[:, 2 * D + 128 * hp:2 * D + 128 * hp + 128]
    cst = _consts(T, D)
    gqT = np.ascontiguousarray(gq.T)
    identb = np.eye(128, dtype=ml_dtypes.bfloat16)

    nc = build_fused(T=T, D=D, H=H, E=E, CAP=CAP, DFF=DFF, NB=B)
    in_maps = []
    HPL = HP // 2
    EH = E // 2
    for c in range(NCORES):
        b = c // 2
        q = c % 2
        exs = [2 * c, 2 * c + 1]
        rsel, eosel, p3sel = _sel_tables(c, T=T, E=E, CAP=CAP, NB=B)
        perm = np.roll(np.arange(E), -EH * q)
        in_maps.append(dict(
            xT=np.ascontiguousarray(x[b].T),
            wqkv_p=np.ascontiguousarray(wq_p[q * HPL:(q + 1) * HPL]),
            wproj=wproj, gqT=np.ascontiguousarray(gqT[:, perm]), p3rsel=p3sel,
            ltemp=lt, identb=identb, rsel=rsel, eosel=eosel,
            osel=_osel_table(c, T, D),
            w1_i=np.ascontiguousarray(w1[exs]), w2_i=np.ascontiguousarray(w2[exs]),
            b1_i=np.ascontiguousarray(b1[exs]), b2_i=np.ascontiguousarray(b2[exs]),
            **cst))
    res = run_bass_kernel_spmd(nc, in_maps, core_ids=list(range(NCORES))).results
    out = np.stack([np.asarray(res[2 * b]["out_o"]) for b in range(B)], axis=0)
    return out.astype(np.float32)


# version 18 (final)
# speedup vs baseline: 1.0351x; 1.0351x over previous
import math
from contextlib import ExitStack

import numpy as np
import concourse.bass as bass
import concourse.mybir as mybir
from concourse.bass_utils import run_bass_kernel_spmd

F32 = mybir.dt.float32
BF16 = mybir.dt.bfloat16
U32 = mybir.dt.uint32
AF = mybir.ActivationFunctionType
TS = mybir.AluOpType
AX = mybir.AxisListType

NCORES = 8
ENGINES = ("t", "v", "s", "g", "q")
DMA_ENG = {"g", "q"}


class Op:
    __slots__ = ("eng", "emit", "deps", "inc", "idx", "tok", "sem", "ring", "prewait", "ind", "scope")

    def __init__(self, eng, emit, deps, inc):
        self.eng, self.emit, self.deps, self.inc = eng, emit, deps, inc


class Sched:

    RING = 20

    def __init__(self):
        self.ops = []
        self.last = {}
        self.scope = None

    def add(self, eng, emit, deps=(), inc=None, ind=False):
        if inc is None:
            inc = 16 if eng in DMA_ENG else 1
        if eng in DMA_ENG and inc == 1:
            ind = True
        deps = [d for d in deps if d is not None]
        op = Op(eng, emit, deps, inc)
        op.scope = self.scope
        op.ind = ind
        op.idx = len(self.ops)
        for d in deps:
            assert d.idx < op.idx, "deps must reference earlier ops"
        if ind is not False:
            chan = 0 if ind is True else int(ind)
            op.ind = chan if op.inc != 1 else "cc"
            lasts = getattr(self, "_last_ind", None)
            if lasts is None:
                lasts = self._last_ind = {}
            key = op.ind
            if lasts.get(key) is not None:
                op.deps = list(op.deps) + [lasts[key]]
            lasts[key] = op
        self.ops.append(op)
        self.last[eng] = op
        return op

    def barrier(self):
        return list(self.last.values())

    def emit_all(self, nc, ctx):
        import os
        lim = int(os.environ.get("SCHED_LIMIT", "0"))
        if lim > 0:
            self.ops = self.ops[:lim]
        sems = {e: ctx.enter_context(nc.semaphore(f"sem_{e}")) for e in ("t", "v", "s")}
        rings = {e: [ctx.enter_context(nc.semaphore(f"ring_{e}{i}"))
                     for i in range(self.RING)] for e in DMA_ENG}
        ind_chans = sorted({op.ind for op in self.ops
                            if op.ind is not False and op.ind != "cc"})
        ind_sems = {ch: ctx.enter_context(nc.semaphore(f"ind_sem{ch}"))
                    for ch in ind_chans}
        cc_sem = ctx.enter_context(nc.semaphore("cc_sem"))
        ind_tot = {ch: 0 for ch in ind_chans}
        cc_tot = 0
        counts = {e: 0 for e in ("t", "v", "s")}
        seq = {e: 0 for e in DMA_ENG}
        slot_tot = {e: [0] * self.RING for e in DMA_ENG}
        for op in self.ops:
            if op.ind is not False:
                if op.ind == "cc":
                    cc_tot += op.inc
                    op.sem = cc_sem
                    op.tok = cc_tot
                else:
                    ind_tot[op.ind] += op.inc
                    op.sem = ind_sems[op.ind]
                    op.tok = ind_tot[op.ind]
                op.ring = None
                op.prewait = 0
            elif op.eng in DMA_ENG:
                r = seq[op.eng] % self.RING
                seq[op.eng] += 1
                op.ring = r
                op.prewait = slot_tot[op.eng][r]
                slot_tot[op.eng][r] += op.inc
                op.sem = rings[op.eng][r]
                op.tok = slot_tot[op.eng][r]
            else:
                counts[op.eng] += op.inc
                op.sem = sems[op.eng]
                op.tok = counts[op.eng]
                op.ring = None
        block = ctx.enter_context(nc.Block())
        for ek in ENGINES:
            mine = [op for op in self.ops if op.eng == ek]
            if not mine:
                continue

            def section(engine, _mine=mine, _ek=ek):
                wm = {}
                cur_scope = [None, None]
                for op in _mine:
                    if op.scope != cur_scope[0]:
                        if cur_scope[0] is not None:
                            nc.leave_named_scope(cur_scope[0], cur_scope[1], False)
                        if op.scope is not None:
                            sid, _ = nc.enter_named_scope(op.scope, False)
                            cur_scope = [op.scope, sid]
                        else:
                            cur_scope = [None, None]
                    if op.ring is not None and op.prewait > 0:
                        key = (id(op.sem),)
                        if wm.get(key, -1) < op.prewait:
                            engine.wait_ge(op.sem, op.prewait)
                            wm[key] = op.prewait
                    for d in op.deps:
                        key = (id(d.sem),)
                        if wm.get(key, -1) < d.tok:
                            engine.wait_ge(d.sem, d.tok)
                            wm[key] = d.tok
                    ins = op.emit(engine)
                    ins.then_inc(op.sem, op.inc)
                if cur_scope[0] is not None:
                    nc.leave_named_scope(cur_scope[0], cur_scope[1], False)

            getattr(block, {"t": "tensor", "v": "vector", "s": "scalar",
                            "g": "gpsimd", "q": "sync"}[ek])(section)
        return sems, counts


def ceil_div(a, b):
    return (a + b - 1) // b


class Arena:

    def __init__(self, nc, ctx, nbytes, name="arena"):
        self.t = ctx.enter_context(nc.sbuf_tensor(name, [128, nbytes], mybir.dt.uint8))
        self.nbytes = nbytes
        self.off = 0
        self.marks = []

    def push(self):
        self.marks.append(self.off)

    def pop(self):
        self.off = self.marks.pop()

    def alloc(self, shape, dt):
        assert shape[0] <= 128
        n = int(np.prod(shape[1:])) * mybir.dt.size(dt)
        off = (self.off + 63) & ~63
        self.off = off + n
        assert self.off <= self.nbytes, f"arena overflow: {self.off} > {self.nbytes}"
        v = self.t[:, off:off + n].bitcast(dt)
        if len(shape) == 2:
            v = v[0:shape[0], :]
        elif len(shape) == 3:
            v = v.rearrange("p (a b) -> p a b", a=shape[1])[0:shape[0], :, :]
        elif len(shape) == 4:
            v = v.rearrange("p (a b c) -> p a b c", a=shape[1], b=shape[2])[0:shape[0]]
        else:
            raise ValueError(shape)
        return v



def build_l1(T=4096, D=1024, H=16, DK=64, E=16, CAP=256, NEG_BIAS=-16.0, BIS_ITERS=36, SINK_ITERS=6, DEBUG_LO=False):
    nc = bass.Bass("TRN2", target_bir_lowering=False, debug=False)
    HP, NT = H // 2, T // 128
    io = {}
    io["xT"] = nc.dram_tensor("xT", [D, T], F32, kind="ExternalInput").ap()
    io["wqkv_p"] = nc.dram_tensor("wqkv_p", [HP, D, 3 * 128], F32, kind="ExternalInput").ap()
    io["wproj"] = nc.dram_tensor("wproj", [D, D], F32, kind="ExternalInput").ap()
    io["gqT"] = nc.dram_tensor("gqT", [D, E], F32, kind="ExternalInput").ap()
    io["ltemp"] = nc.dram_tensor("ltemp", [1, 1], F32, kind="ExternalInput").ap()
    io["identf"] = nc.dram_tensor("identf", [128, 128], F32, kind="ExternalInput").ap()
    io["masks"] = nc.dram_tensor("masks", [128, 4, 512], BF16, kind="ExternalInput").ap()
    io["iota256"] = nc.dram_tensor("iota256", [128, 256], F32, kind="ExternalInput").ap()
    io["iotaN"] = nc.dram_tensor("iotaN", [128, NT], F32, kind="ExternalInput").ap()
    io["iotaNhi"] = nc.dram_tensor("iotaNhi", [128, NT], BF16, kind="ExternalInput").ap()
    io["iotaNlo"] = nc.dram_tensor("iotaNlo", [128, NT], BF16, kind="ExternalInput").ap()
    io["ones_in"] = nc.dram_tensor("ones_in", [128, 128], F32, kind="ExternalInput").ap()
    io["xmid_f"] = nc.dram_tensor("xmid_f", [T, D], F32, kind="ExternalOutput").ap()
    io["xmid_b"] = nc.dram_tensor("xmid_b", [T, D], BF16, kind="ExternalOutput").ap()
    io["idx_o"] = nc.dram_tensor("idx_o", [E, CAP], U32, kind="ExternalOutput").ap()
    io["gate_o"] = nc.dram_tensor("gate_o", [E, CAP], BF16 if False else F32, kind="ExternalOutput").ap()
    oacc_t = nc.dram_tensor("oacc_h", [HP, 128, T], BF16).ap()
    io["oacc_h"] = [oacc_t[hp].rearrange("p (j f) -> p j f", f=512)
                    for hp in range(HP)]
    io["oacc_flat"] = oacc_t.rearrange("h p t -> p h t")
    sch = Sched()
    with ExitStack() as ctx:
        sb = lambda name, shape, dt: ctx.enter_context(nc.sbuf_tensor(name, shape, dt))
        ps = mk_psum(nc, ctx)
        st = emit_l1(nc, sch, sb, ps, io, T=T, D=D, H=H, E=E, CAP=CAP,
                     NEG_BIAS=NEG_BIAS, BIS_ITERS=BIS_ITERS, SINK_ITERS=SINK_ITERS)
        fin = sch.add("g", lambda e: e.nop(), inc=1, deps=st["ln_dma"] + st["odl"])
        sch.emit_all(nc, ctx)
    return nc


def mk_psum(nc, ctx):
    wa = ctx.enter_context(nc.psum_tensor("pswa", [128, 1024], F32))
    wb = ctx.enter_context(nc.psum_tensor("pswb", [128, 1024], F32))
    ps = [wa[:, 0:512], wa[:, 512:1024], wb[:, 0:512], wb[:, 512:1024]]
    for i in range(4, 8):
        ps.append(ctx.enter_context(nc.psum_tensor(f"ps{i}", [128, 512], F32))[:, :])
    ps.append(wa[:, :])
    ps.append(wb[:, :])
    return ps


def emit_l1(nc, sch, sb, ps, io, T=4096, D=1024, H=16, E=16, CAP=256,
            NEG_BIAS=-16.0, BIS_ITERS=36, SINK_ITERS=6,
            HPL=None, oag=None, oall_rows=None, sel_experts=None):
    assert D % 128 == 0 and T % 512 == 0 and H % 2 == 0 and E == 16
    DC = D // 128
    HP = H // 2
    NT = T // 128
    if HPL is None:
        HPL = HP
    EH = E if sel_experts is None else sel_experts
    NJ = T // 512
    NSL = ceil_div(CAP, 128)
    assert NT * E <= 512, "sinkhorn free dim must fit one op"
    xT = io["xT"]; wqkv_p = io["wqkv_p"]; wproj = io["wproj"]; gqT = io["gqT"]
    ltemp = io["ltemp"]; identf = io["identf"]; masks = io["masks"]
    iota256 = io["iota256"]; iotaN = io["iotaN"]; ones_in = io["ones_in"]
    iotaNhi = io["iotaNhi"]; iotaNlo = io["iotaNlo"]
    xmid_f = io["xmid_f"]; xmid_b = io["xmid_b"]; idx_o = io["idx_o"]
    gate_o = io["gate_o"]; oacc_h = io["oacc_h"]
    oacc_flat = io.get("oacc_flat"); osel_i = io.get("osel")
    if True:
        xTb = sb("xTb", [128, DC, T], BF16)
        wqp = sb("wqp", [128, DC, 384], BF16)
        kT = sb("kT", [128, T], BF16)
        qT = sb("qT", [128, T], BF16)
        vsb = sb("vsb", [128, NT, 130], BF16)
        attb = sb("attb", [128, 2, 1024], BF16)
        zb = sb("zb", [65, 512], F32)
        zbc = sb("zbc", [128, 512], F32)
        otile = sb("otile", [128, 2, 512], BF16)
        wpj = sb("wpj", [128, DC, D], BF16)
        if T >= 3072:
            oin_v = lambda db: xTb[:, :, 2048 + 512 * db:2048 + 512 * db + 512]
            oin_c = lambda db, cc: xTb[:, cc, 2048 + 512 * db:2048 + 512 * db + 512]
            xf_v = lambda db: xTb[:, :, :].bitcast(F32)[:, :, 512 * db:512 * db + 512]
            xf_c = lambda db, c: xTb[:, c, :].bitcast(F32)[:, 512 * db:512 * db + 512]
        else:
            oin_t = sb("oin_t", [128, 2, DC, 512], BF16)
            xf_t = sb("xf_t", [128, 2, DC, 512], F32)
            oin_v = lambda db: oin_t[:, db, :, :]
            oin_c = lambda db, cc: oin_t[:, db, cc, :]
            xf_v = lambda db: xf_t[:, db, :, :]
            xf_c = lambda db, c: xf_t[:, db, c, :]
        gq_s = sb("gq_s", [128, DC, E], F32)
        rT = sb("rT", [128, DC, 512], F32)
        rrow = sb("rrow", [128, 2, D], F32)
        xrowb = sb("xrowb", [128, 2, D], BF16)
        stats = sb("stats", [128, 8], F32)
        logit = sb("logit", [128, NT, E], F32)
        esb = sb("esb", [128, NT * E], F32)
        sqs = sb("sqs", [128, D], F32)
        row1 = sb("row1", [1, 512], F32)
        red16 = sb("red16", [1, 16], F32)
        bc16 = sb("bc16", [128, 16], F32)
        g0bc = sb("g0bc", [128, E], F32)
        ltc = sb("ltc", [128, 1], F32)
        lo_t = sb("lo_t", [128, 16], F32)
        hi_t = sb("hi_t", [128, 16], F32)
        mid_t = sb("mid_t", [128, 16], F32)
        cmpb = sb("cmpb", [128, NT * E], BF16)
        mtok = sb("mtok", [128, NT, E], F32)
        if T >= 3072:
            maskT = xTb[:, 4:6, :].bitcast(F32).rearrange("p a b -> p (a b)")[0:16, 0:T]
            c0 = xTb[:, 6:8, :].bitcast(F32).rearrange("p a b -> p (a b)")[0:16, 0:T]
        else:
            maskT = sb("maskT", [16, T], F32)
            c0 = sb("c0", [16, T], F32)
        ctok = sb("ctok", [128, NT, E], F32)
        rhs35 = sb("rhs35", [128, NT, 35], BF16)
        ohb = sb("ohb", [128, 2, 256], BF16)
        ohsb = sb("ohsb", [35, 2, 256], F32)
        pyb = sb("pyb", [128, 2, 35], F32)
        ahib = sb("ahib", [128, NT * E], BF16)
        ioNhl = sb("ioNhl", [128, 2, NT], BF16)
        smallf = sb("smallf", [128, 4], F32)
        idx_sb = sb("idx_sb", [128, E, NSL], U32)
        gate_sb = sb("gate_sb", [128, E, NSL], F32)
        identsb = sb("identsb", [128, 128], F32)
        if oall_rows is not None:
            osel_sb = sb("osel_sb", [128, DC, NJ], U32)
        masksb = sb("masksb", [128, 4, 512], BF16)
        io256 = sb("io256", [128, 256], F32)
        onescf = sb("onescf", [128, 128], F32)
        onescb = sb("onescb", [128, 1], BF16)
        negb = sb("negb", [128, 1], F32)
        A = {}
        A["ident"] = sch.add("q", lambda e: e.dma_start(out=identsb[:, :], in_=identf[:, :]))
        A["masks"] = sch.add("q", lambda e: e.dma_start(out=masksb[:, :, :], in_=masks[:, :, :]))
        A["io256"] = sch.add("q", lambda e: e.dma_start(out=io256[:, :], in_=iota256[:, :]))
        A["ones"] = sch.add("q", lambda e: e.dma_start(out=onescf[:, :], in_=ones_in[:, :]))
        A["gq"] = sch.add("q", lambda e: e.dma_start(
            out=gq_s[:, :, :], in_=gqT.rearrange("(c p) e -> p c e", p=128)))
        A["lt"] = sch.add("q", lambda e: e.dma_start(out=stats[0:1, 7:8], in_=ltemp[:, :]))
        A["osel"] = (sch.add("q", lambda e: e.dma_start(out=osel_sb[:, :, :],
                                                        in_=osel_i[:, :, :]))
                     if oall_rows is not None else None)
        A["onescb"] = sch.add("v", lambda e: e.memset(onescb[:, :], 1.0), deps=())
        A["negb"] = sch.add("v", lambda e: e.memset(negb[:, :], NEG_BIAS), deps=())
        xtb_deps = []
        for c in range(DC):
            xtb_deps.append(sch.add("g", lambda e, c=c: e.dma_start(
                out=xTb[:, c, :], in_=xT[128 * c:128 * c + 128, :])))
        A["xTb"] = xtb_deps
        wpj_deps = []
        for c in range(DC):
            wpj_deps.append(sch.add("g", lambda e, c=c: e.dma_start(
                out=wpj[:, c, :], in_=wproj[128 * c:128 * c + 128, :])))
        A["wpj"] = wpj_deps
        A["vones"] = [
            sch.add("v", lambda e: e.memset(vsb[:, :, 64:65], 1.0)),
            sch.add("v", lambda e: e.memset(vsb[:, :, 129:130], 1.0)),
        ]
        elt = sch.add("s", lambda e: e.activation(out=stats[0:1, 6:7], in_=stats[0:1, 7:8],
                                                  func=AF.Exp, scale=-1.0), deps=[A["lt"]])
        eltm = sch.add("t", lambda e: e.matmul(ps[7][0:128, 0:1], onescf[0:1, 0:128],
                                               stats[0:1, 6:7], start=True, stop=True),
                       deps=[elt, A["ones"]])
        A["ltc"] = sch.add("v", lambda e: e.tensor_copy(out=ltc[:, :], in_=ps[7][0:128, 0:1]),
                           deps=[eltm])
        g0acc = None
        for c in range(DC):
            g0acc = sch.add("t", lambda e, c=c, s=(c == 0), p=(c == DC - 1): e.matmul(
                ps[6][0:1, 0:E], onescf[:, 0:1], gq_s[:, c, :], start=s, stop=p),
                deps=[A["gq"], A["ones"]] + ([A["ltc"]] if c == 0 else []))
        g0row = sch.add("v", lambda e: e.tensor_copy(out=red16[:, :], in_=ps[6][0:1, 0:E]),
                        deps=[g0acc])
        g0m = sch.add("t", lambda e: e.matmul(ps[6][0:128, 0:E], onescf[0:1, 0:128],
                                              red16[0:1, :], start=True, stop=True),
                      deps=[g0row])
        A["g0bc"] = sch.add("v", lambda e: e.tensor_copy(out=g0bc[:, :], in_=ps[6][0:128, 0:E]),
                            deps=[g0m])

        base_scope = sch.scope
        otile_rd = [None, None]
        kqv_last_use = None
        ps_user = [None] * 8
        ps_user[6] = A["g0bc"]
        ps_user[7] = A["ltc"]
        att_rd = [None, None]
        oacc_wr = {}

        ag_ops = {}
        for hp in range(HPL):
            sch.scope = f"{base_scope}_qkv"
            wq_dep = sch.add("g", lambda e, hp=hp: e.dma_start(
                out=wqp[:, :, :],
                in_=wqkv_p[hp, :, :].rearrange("(c p) w -> p c w", p=128)),
                deps=[kqv_last_use] if kqv_last_use else [])
            k_ev = {}
            q_ev = {}
            v_ev = {}
            for tj8 in range(NJ):
                b = tj8 % 2
                acc = None
                for c in range(DC):
                    acc = sch.add("t", lambda e, c=c, tj8=tj8, b=b, s=(c == 0), p=(c == DC - 1):
                                  e.matmul(ps[b][0:128, :], wqp[:, c, 128:256],
                                           xTb[:, c, 512 * tj8:512 * tj8 + 512],
                                           start=s, stop=p),
                                  deps=([wq_dep] + A["xTb"] + ([ps_user[b]] if ps_user[b] else [])) if c == 0 else [])
                k_ev[tj8] = sch.add("v", lambda e, tj8=tj8, b=b: e.tensor_copy(
                    out=kT[:, 512 * tj8:512 * tj8 + 512], in_=ps[b][0:128, :]), deps=[acc])
                ps_user[b] = k_ev[tj8]
            for tj8 in range(NJ):
                b = tj8 % 2
                acc = None
                for c in range(DC):
                    acc = sch.add("t", lambda e, c=c, tj8=tj8, b=b, s=(c == 0), p=(c == DC - 1):
                                  e.matmul(ps[b][0:128, :], wqp[:, c, 0:128],
                                           xTb[:, c, 512 * tj8:512 * tj8 + 512],
                                           start=s, stop=p),
                                  deps=[ps_user[b]] if c == 0 else [])
                q_ev[tj8] = sch.add("v", lambda e, tj8=tj8, b=b: e.tensor_copy(
                    out=qT[:, 512 * tj8:512 * tj8 + 512], in_=ps[b][0:128, :]), deps=[acc])
                ps_user[b] = q_ev[tj8]
            for sbk in range(NT):
                b = sbk % 2
                acc = None
                for c in range(DC):
                    acc = sch.add("t", lambda e, c=c, sbk=sbk, b=b, s=(c == 0), p=(c == DC - 1):
                                  e.matmul(ps[b][0:128, 0:128], xTb[:, c, 128 * sbk:128 * sbk + 128],
                                           wqp[:, c, 256:384], start=s, stop=p),
                                  deps=[ps_user[b]] if c == 0 else [])
                ev0 = sch.add("v", lambda e, sbk=sbk, b=b: e.tensor_copy(
                    out=vsb[:, sbk, 0:64], in_=ps[b][0:128, 0:64]),
                    deps=[acc] + (A["vones"] if sbk == 0 else []))
                ev1 = sch.add("v", lambda e, sbk=sbk, b=b: e.tensor_copy(
                    out=vsb[:, sbk, 65:129], in_=ps[b][0:128, 64:128]), deps=[acc])
                v_ev[sbk] = ev1
                ps_user[b] = ev1

            sch.scope = f"{base_scope}_att"
            for tj in range(NJ):
                nsb = 4 * (tj + 1)

                def emit_S(si, tj=tj):
                    par = si % 2
                    kdep = k_ev[max(0, (128 * si + 127) // 512)]
                    mm_e = sch.add("t", lambda e, si=si, tj=tj, par=par: e.matmul(
                        ps[2 * par][0:128, :], kT[0:64, 128 * si:128 * si + 128],
                        qT[0:64, 512 * tj:512 * tj + 512], start=True, stop=True),
                        deps=[kdep, q_ev[tj], ps_user[2 * par]])
                    mm_o = sch.add("t", lambda e, si=si, tj=tj, par=par: e.matmul(
                        ps[2 * par + 1][0:128, :], kT[64:128, 128 * si:128 * si + 128],
                        qT[64:128, 512 * tj:512 * tj + 512], start=True, stop=True),
                        deps=[kdep, ps_user[2 * par + 1]])
                    return [mm_e, mm_o]

                pend = {0: emit_S(0)}
                avz_first = True
                for si in range(nsb):
                    par = si % 2
                    if si + 1 < nsb:
                        pend[si + 1] = emit_S(si + 1)
                    mm = pend.pop(si)
                    ex = sch.add("s", lambda e, par=par: e.activation(
                        out=attb[:, par, :], in_=ps[8 + par][0:128, :], func=AF.Exp,
                        scale=0.125, bias=negb[:, 0:1]),
                        deps=mm + [att_rd[par], A["negb"]])
                    ps_user[2 * par] = ex
                    ps_user[2 * par + 1] = ex
                    if si >= 4 * tj:
                        r = si - 4 * tj
                        ex = sch.add("v", lambda e, par=par, r=r: e.tensor_mul(
                            attb[:, par, :].rearrange("p (h f) -> p h f", h=2),
                            attb[:, par, :].rearrange("p (h f) -> p h f", h=2),
                            masksb[:, r, :].unsqueeze(1).to_broadcast([128, 2, 512])),
                            deps=[ex, A["masks"]])
                    st = avz_first
                    fin_ = (si == nsb - 1)
                    av_e = sch.add("t", lambda e, si=si, par=par, st=st, fin_=fin_: e.matmul(
                        ps[4][0:65, :], vsb[:, si, 0:65], attb[:, par, 0:512],
                        start=st, stop=fin_), deps=[ex, v_ev[si]] +
                        ([ps_user[4]] if st and ps_user[4] else []))
                    av_o = sch.add("t", lambda e, si=si, par=par, st=st, fin_=fin_: e.matmul(
                        ps[5][64:128, :], vsb[:, si, 65:129], attb[:, par, 512:1024],
                        start=st, stop=fin_, tile_position=(0, 64)), deps=[ex] +
                        ([ps_user[5]] if st and ps_user[5] else []))
                    z_o = sch.add("t", lambda e, si=si, par=par, st=st, fin_=fin_: e.matmul(
                        ps[7][32:33, :], vsb[:, si, 129:130], attb[:, par, 512:1024],
                        start=st, stop=fin_, tile_position=(0, 32)), deps=[ex] +
                        ([ps_user[7]] if st and ps_user[7] else []))
                    avz_first = False
                    att_rd[par] = z_o
                    av_last = [av_e, av_o, z_o]
                cz0 = sch.add("v", lambda e: e.tensor_copy(out=zb[64:65, :],
                                                            in_=ps[4][64:65, :]),
                              deps=av_last)
                cz1 = sch.add("v", lambda e: e.tensor_copy(out=zb[32:33, :], in_=ps[7][32:33, :]),
                              deps=av_last)
                ps_user[7] = cz1
                r0 = sch.add("v", lambda e: e.reciprocal(zb[64:65, :], zb[64:65, :]), deps=[cz0])
                r1 = sch.add("v", lambda e: e.reciprocal(zb[32:33, :], zb[32:33, :]), deps=[cz1])
                b0 = sch.add("t", lambda e: e.matmul(ps[0][0:64, :], onescf[64:65, 0:64],
                                                     zb[64:65, :], start=True, stop=True,
                                                     tile_position=(64, 0)),
                             deps=[r0, A["ones"], ps_user[0]])
                b1 = sch.add("t", lambda e: e.matmul(ps[0][64:128, :], onescf[32:33, 0:64],
                                                     zb[32:33, :], start=True, stop=True,
                                                     tile_position=(32, 64)), deps=[r1])
                zev = sch.add("v", lambda e: e.tensor_copy(out=zbc[:, :], in_=ps[0][0:128, :]),
                              deps=[b0, b1])
                ps_user[0] = zev
                ob = tj % 2
                on1 = sch.add("v", lambda e, ob=ob: e.tensor_mul(
                    otile[0:64, ob, :], ps[4][0:64, :], zbc[0:64, :]),
                    deps=[zev] + ([otile_rd[ob]] if otile_rd[ob] else []))
                on2 = sch.add("v", lambda e, ob=ob: e.tensor_mul(
                    otile[64:128, ob, :], ps[5][64:128, :], zbc[64:128, :]),
                    deps=[zev, on1])
                ps_user[4] = on1
                ps_user[5] = on2
                odma = sch.add("q", lambda e, hp=hp, tj=tj, ob=ob: e.dma_start(
                    out=oacc_h[hp][:, tj, :], in_=otile[:, ob, :]),
                    deps=[on1, on2])
                otile_rd[ob] = odma
                oacc_wr[(hp, tj)] = odma
            kqv_last_use = av_last[-1]
            if oag is not None:
                ag_ops[hp] = oag(hp, [oacc_wr[(hp, tj)] for tj in range(NJ)])


        sch.scope = f"{base_scope}_projln"
        lastC = kqv_last_use
        oin_rd = [None, None]
        xf_rd = [None, None]
        rT_rd = None
        rrow_rd = [None, None]
        xrowb_rd = [None, None]
        stats_rd = None
        logit_wr = []
        ln_dma = []
        for tj in range(NJ):
            db = tj % 2
            if oall_rows is None:
                oin_ld = [sch.add("q", lambda e, tj=tj, db=db: e.dma_start(
                    out=oin_v(db),
                    in_=oacc_flat[:, :, 512 * tj:512 * tj + 512]),
                    deps=[oacc_wr[(hp2, tj)] for hp2 in range(HPL)] + [lastC] +
                         ([oin_rd[db]] if oin_rd[db] else []))] * DC
            else:
                oin_ld = []
                for cc in range(DC):
                    oin_ld.append(sch.add("g", lambda e, tj=tj, db=db, cc=cc:
                                          e.indirect_dma_start(
                        out=oin_c(db, cc), out_offset=None,
                        in_=oall_rows[cc % HPL],
                        in_offset=bass.IndirectOffsetOnAxis(
                            osel_sb[:, cc, tj:tj + 1], 0)),
                        deps=[ag_ops[cc % HPL], A["osel"]] +
                             ([oin_rd[db]] if oin_rd[db] and cc == 0 else [])))
            xf_ld = sch.add("q", lambda e, tj=tj, db=db: e.dma_start(
                out=xf_v(db),
                in_=xT.rearrange("(c p) t -> p c t", p=128)[:, :, 512 * tj:512 * tj + 512]),
                deps=[lastC] + ([xf_rd[db]] if xf_rd[db] else []))
            rT_parts = []
            for c in range(DC):
                b = c % 2
                acc = None
                for cc in range(DC):
                    acc = sch.add("t", lambda e, c=c, cc=cc, b=b, db=db, s=(cc == 0), p=(cc == DC - 1):
                                  e.matmul(ps[b][0:128, :], wpj[:, cc, 128 * c:128 * c + 128],
                                           oin_c(db, cc),
                                           start=s, stop=p),
                                  deps=([oin_ld[cc], ps_user[b]] + A["wpj"] +
                                        ([lastC] if lastC else [])) if cc == 0
                                  else [oin_ld[cc]])
                radd = sch.add("v", lambda e, c=c, b=b, db=db: e.tensor_add(
                    rT[:, c, :], ps[b][0:128, :], xf_c(db, c)),
                    deps=[acc, xf_ld] + ([rT_rd] if rT_rd and c == 0 else []))
                ps_user[b] = radd
                rT_parts.append(radd)
            for sub in range(4):
                rb = sub % 2
                l0acc = None
                for c in range(DC):
                    l0acc = sch.add("t", lambda e, c=c, sub=sub, s=(c == 0), p=(c == DC - 1):
                                    e.matmul(ps[2][0:128, 0:E], rT[:, c, 128 * sub:128 * sub + 128],
                                             gq_s[:, c, :], start=s, stop=p),
                                    deps=(rT_parts + [ps_user[2], A["gq"]]) if c == 0 else [])
                tp_ev = []
                for c in range(DC):
                    tp = sch.add("t", lambda e, c=c, sub=sub: e.transpose(
                        ps[3][0:128, 0:128], rT[:, c, 128 * sub:128 * sub + 128], identsb[:, :]),
                        deps=[l0acc, A["ident"]] if c == 0 else [ps_user[3]])
                    ev = sch.add("v", lambda e, c=c, rb=rb: e.tensor_copy(
                        out=rrow[:, rb, 128 * c:128 * c + 128], in_=ps[3][0:128, 0:128]),
                        deps=[tp] + ([rrow_rd[rb]] if rrow_rd[rb] and c == 0 else []))
                    ps_user[3] = ev
                    tp_ev.append(ev)
                c1 = sch.add("v", lambda e, rb=rb: e.reduce_sum(
                    stats[:, 0:1], rrow[:, rb, :], axis=AX.X),
                    deps=tp_ev + ([stats_rd] if stats_rd else []))
                c2 = sch.add("v", lambda e: e.tensor_scalar(
                    out=stats[:, 0:1], in0=stats[:, 0:1], scalar1=1.0 / D, scalar2=None,
                    op0=TS.mult), deps=[c1])
                sqv = sch.add("v", lambda e, rb=rb: e.tensor_mul(
                    sqs[:, :], rrow[:, rb, :], rrow[:, rb, :]),
                    deps=tp_ev + ([stats_rd] if stats_rd else []))
                c4 = sch.add("v", lambda e: e.reduce_sum(
                    stats[:, 1:2], sqs[:, :], axis=AX.X), deps=[sqv])
                c5 = sch.add("v", lambda e: e.tensor_scalar(
                    out=stats[:, 1:2], in0=stats[:, 1:2], scalar1=1.0 / D, scalar2=None,
                    op0=TS.mult), deps=[c4])
                c6 = sch.add("v", lambda e: e.tensor_mul(
                    stats[:, 2:3], stats[:, 0:1], stats[:, 0:1]), deps=[c2])
                c7 = sch.add("v", lambda e: e.tensor_tensor(
                    out=stats[:, 1:2], in0=stats[:, 1:2], in1=stats[:, 2:3], op=TS.subtract),
                    deps=[c5, c6])
                c8 = sch.add("v", lambda e: e.tensor_scalar(
                    out=stats[:, 1:2], in0=stats[:, 1:2], scalar1=1e-5, scalar2=None,
                    op0=TS.add), deps=[c7])
                c9 = sch.add("s", lambda e: e.activation(
                    out=stats[:, 3:4], in_=stats[:, 1:2], func=AF.Sqrt), deps=[c8])
                c10 = sch.add("v", lambda e: e.reciprocal(stats[:, 4:5], stats[:, 3:4]),
                              deps=[c9])
                xm = sch.add("v", lambda e, rb=rb: e.tensor_scalar(
                    out=rrow[:, rb, :], in0=rrow[:, rb, :], scalar1=stats[:, 0:1],
                    scalar2=stats[:, 4:5], op0=TS.subtract, op1=TS.mult),
                    deps=[c10, sqv])
                xb = sch.add("v", lambda e, rb=rb: e.tensor_copy(
                    out=xrowb[:, rb, :], in_=rrow[:, rb, :]),
                    deps=[xm] + ([xrowb_rd[rb]] if xrowb_rd[rb] else []))
                row0 = 512 * tj + 128 * sub
                d1 = sch.add("q", lambda e, rb=rb, row0=row0: e.dma_start(
                    out=xmid_f[row0:row0 + 128, :], in_=rrow[:, rb, :]), deps=[xm])
                d2 = sch.add("q", lambda e, rb=rb, row0=row0: e.dma_start(
                    out=xmid_b[row0:row0 + 128, :], in_=xrowb[:, rb, :]), deps=[xb])
                ln_dma += [d1, d2]
                rrow_rd[rb] = d1
                xrowb_rd[rb] = d2
                lg1 = sch.add("v", lambda e: e.tensor_scalar(
                    out=bc16[:, :], in0=g0bc[:, :], scalar1=stats[:, 0:1], scalar2=None,
                    op0=TS.mult), deps=[c10, A["g0bc"]])
                lg2 = sch.add("v", lambda e, tj=tj, sub=sub: e.tensor_tensor(
                    out=logit[:, 4 * tj + sub, :], in0=ps[2][0:128, 0:E], in1=bc16[:, :],
                    op=TS.subtract), deps=[lg1, l0acc])
                ps_user[2] = lg2
                lg3 = sch.add("v", lambda e: e.tensor_mul(
                    stats[:, 5:6], stats[:, 4:5], ltc[:, :]), deps=[c10, A["ltc"]])
                lg4 = sch.add("v", lambda e, tj=tj, sub=sub: e.tensor_scalar(
                    out=logit[:, 4 * tj + sub, :], in0=logit[:, 4 * tj + sub, :],
                    scalar1=stats[:, 5:6], scalar2=None, op0=TS.mult), deps=[lg2, lg3])
                logit_wr.append(lg4)
                stats_rd = lg4
            oin_rd[db] = rT_parts[-1]
            xf_rd[db] = rT_parts[-1]
            rT_rd = ps_user[3]

        sch.scope = f"{base_scope}_sink"
        prev = logit_wr[:]
        lgview = logit
        for it in range(SINK_ITERS):
            ex1 = sch.add("s", lambda e: e.activation(
                out=esb[:, 0:NT * E], in_=lgview.rearrange("p n e -> p (n e)")[:, :],
                func=AF.Exp), deps=prev)
            red1 = sch.add("v", lambda e: e.reduce_sum(
                mtok[:, :, 0:1], esb.rearrange("p (n e) -> p n e", e=E)[:, :, :],
                axis=AX.X), deps=[ex1])
            ln1 = sch.add("s", lambda e: e.activation(
                out=mtok[:, :, 1:2], in_=mtok[:, :, 0:1], func=AF.Ln), deps=[red1])
            sub1 = sch.add("v", lambda e: e.tensor_tensor(
                out=lgview[:, :, :], in0=lgview[:, :, :],
                in1=mtok[:, :, 1:2].to_broadcast([128, NT, E]), op=TS.subtract),
                deps=[ln1])
            ex2 = sch.add("s", lambda e: e.activation(
                out=esb[:, 0:NT * E], in_=lgview.rearrange("p n e -> p (n e)")[:, :],
                func=AF.Exp), deps=[sub1])
            cs1 = sch.add("t", lambda e: e.matmul(
                ps[2][0:1, 0:NT * E], onescf[:, 0:1], esb[:, 0:NT * E],
                start=True, stop=True), deps=[ex2, ps_user[2]])
            ev1 = sch.add("v", lambda e: e.tensor_copy(out=row1[:, 0:NT * E],
                                                       in_=ps[2][0:1, 0:NT * E]), deps=[cs1])
            ps_user[2] = ev1
            red2 = sch.add("v", lambda e: e.reduce_sum(
                red16[:, :].unsqueeze(2),
                row1[:, 0:NT * E].rearrange("a (n e) -> a e n", e=E)[:, :, :], axis=AX.X), deps=[ev1])
            ln2 = sch.add("s", lambda e: e.activation(
                out=row1[0:1, 0:E], in_=red16[:, :], func=AF.Ln), deps=[red2])
            bc1 = sch.add("t", lambda e: e.matmul(
                ps[2][0:128, 0:E], onescf[0:1, 0:128], row1[0:1, 0:E],
                start=True, stop=True), deps=[ln2, ps_user[2]])
            ev2 = sch.add("v", lambda e: e.tensor_copy(out=bc16[:, 0:E],
                                                       in_=ps[2][0:128, 0:E]), deps=[bc1])
            ps_user[2] = ev2
            sub2 = sch.add("v", lambda e: e.tensor_tensor(
                out=lgview[:, :, :], in0=lgview[:, :, :],
                in1=bc16[:, :].unsqueeze(1).to_broadcast([128, NT, E]), op=TS.subtract),
                deps=[ev2])
            prev = [sub2]
        sink_done = prev[0]
        aff = sch.add("s", lambda e: e.activation(
            out=esb[:, 0:NT * E], in_=lgview.rearrange("p n e -> p (n e)")[:, :],
            func=AF.Exp), deps=[sink_done])

        sch.scope = f"{base_scope}_bis"
        i_lo = sch.add("v", lambda e: e.memset(lo_t[:, :], -60.0), deps=[sink_done])
        i_hi = sch.add("v", lambda e: e.memset(hi_t[:, :], 0.0), deps=[sink_done])
        pv = [i_lo, i_hi]
        for it in range(BIS_ITERS):
            m1 = sch.add("v", lambda e: e.tensor_add(mid_t[:, :], lo_t[:, :], hi_t[:, :]),
                         deps=pv)
            m2 = sch.add("v", lambda e: e.tensor_scalar(
                out=mid_t[:, :], in0=mid_t[:, :], scalar1=0.5, scalar2=None, op0=TS.mult),
                deps=[m1])
            cp = sch.add("v", lambda e: e.tensor_tensor(
                out=cmpb.rearrange("p (n e) -> p n e", e=E)[:, :, :], in0=lgview[:, :, :],
                in1=mid_t[:, :].unsqueeze(1).to_broadcast([128, NT, E]), op=TS.is_gt), deps=[m2])
            ct = sch.add("t", lambda e: e.matmul(
                ps[2][0:1, 0:NT * E], onescb[:, 0:1], cmpb[:, 0:NT * E],
                start=True, stop=True), deps=[cp, ps_user[2], A["onescb"]])
            ev = sch.add("v", lambda e: e.tensor_copy(out=row1[:, 0:NT * E],
                                                      in_=ps[2][0:1, 0:NT * E]), deps=[ct])
            ps_user[2] = ev
            rd = sch.add("v", lambda e: e.reduce_sum(
                red16[:, :].unsqueeze(2),
                row1[:, 0:NT * E].rearrange("a (n e) -> a e n", e=E)[:, :, :], axis=AX.X), deps=[ev])
            cg = sch.add("v", lambda e: e.tensor_scalar(
                out=red16[:, :], in0=red16[:, :], scalar1=float(CAP) - 0.5, scalar2=None,
                op0=TS.is_gt), deps=[rd])
            bc = sch.add("t", lambda e: e.matmul(
                ps[2][0:128, 0:E], onescf[0:1, 0:128], red16[0:1, :],
                start=True, stop=True), deps=[cg, ps_user[2]])
            ev2 = sch.add("v", lambda e: e.tensor_copy(out=bc16[:, 0:E],
                                                       in_=ps[2][0:128, 0:E]), deps=[bc])
            ps_user[2] = ev2
            ncge = sch.add("v", lambda e: e.tensor_scalar(
                out=mtok[:, 3, :], in0=bc16[:, 0:E], scalar1=-1.0,
                scalar2=1.0, op0=TS.mult, op1=TS.add), deps=[ev2])
            t1 = sch.add("v", lambda e: e.tensor_mul(mtok[:, 0, :], mid_t[:, :], bc16[:, 0:E]),
                         deps=[ev2])
            t2 = sch.add("v", lambda e: e.tensor_mul(mtok[:, 1, :], lo_t[:, :], mtok[:, 3, :]),
                         deps=[ncge])
            s1 = sch.add("v", lambda e: e.tensor_add(lo_t[:, :], mtok[:, 0, :], mtok[:, 1, :]),
                         deps=[t1, t2])
            t3 = sch.add("v", lambda e: e.tensor_mul(mtok[:, 2, :], hi_t[:, :], bc16[:, 0:E]),
                         deps=[ev2, t2])
            t4 = sch.add("v", lambda e: e.tensor_mul(mtok[:, 1, :], mid_t[:, :], mtok[:, 3, :]),
                         deps=[s1])
            s2 = sch.add("v", lambda e: e.tensor_add(hi_t[:, :], mtok[:, 2, :], mtok[:, 1, :]),
                         deps=[t3, t4])
            pv = [s1, s2]
        bis_done = pv

        sch.scope = f"{base_scope}_sel"
        g1 = sch.add("v", lambda e: e.tensor_tensor(
            out=mtok[:, :, :], in0=lgview[:, :, :],
            in1=lo_t[:, :].unsqueeze(1).to_broadcast([128, NT, E]), op=TS.is_gt),
            deps=bis_done + [sink_done])
        mt_ev = []
        for ntk in range(NT):
            tp = sch.add("t", lambda e, ntk=ntk: e.transpose(
                ps[3][0:E, 0:128], mtok[:, ntk, :], identsb[:, :]),
                deps=[g1, ps_user[3]] if ntk == 0 else [ps_user[3]])
            ev = sch.add("v", lambda e, ntk=ntk: e.tensor_copy(
                out=maskT[:, 128 * ntk:128 * ntk + 128], in_=ps[3][0:E, 0:128]), deps=[tp])
            ps_user[3] = ev
            mt_ev.append(ev)
        scan = sch.add("v", lambda e: e.tensor_tensor_scan(
            out=c0[:, :], data0=maskT[:, :],
            data1=onescf[0:16, 0:1].to_broadcast([16, T]), initial=0.0,
            op0=TS.add, op1=TS.mult), deps=mt_ev + [A["ones"]])
        csel = sch.add("v", lambda e: e.tensor_mul(c0[:, :], c0[:, :], maskT[:, :]),
                       deps=[scan])
        ct_ev = []
        for ntk in range(NT):
            tp = sch.add("t", lambda e, ntk=ntk: e.transpose(
                ps[3][0:128, 0:E], c0[:, 128 * ntk:128 * ntk + 128], identsb[0:16, 0:16]),
                deps=[csel, ps_user[3]] if ntk == 0 else [ps_user[3]])
            ev = sch.add("v", lambda e, ntk=ntk: e.tensor_copy(
                out=ctok[:, ntk, :], in_=ps[3][0:128, 0:E]), deps=[tp])
            ps_user[3] = ev
            ct_ev.append(ev)
        NW = 3 + 2 * E
        rbA = sch.add("q", lambda e: e.dma_start(out=ioNhl[:, 0, :], in_=iotaNhi[:, :]))
        rbB = sch.add("q", lambda e: e.dma_start(out=ioNhl[:, 1, :], in_=iotaNlo[:, :]))
        rb1 = sch.add("v", lambda e: e.tensor_copy(out=rhs35[:, :, 0:1],
                                                   in_=ioNhl[:, 0, :].unsqueeze(2)),
                      deps=[rbA])
        rb2 = sch.add("v", lambda e: e.tensor_copy(out=rhs35[:, :, 1:2],
                                                   in_=ioNhl[:, 1, :].unsqueeze(2)),
                      deps=[rbB])
        rb3 = sch.add("v", lambda e: e.memset(rhs35[:, :, 2:3], 1.0))
        ahi = sch.add("v", lambda e: e.tensor_copy(out=ahib[:, 0:NT * E],
                                                   in_=esb[:, 0:NT * E]), deps=[aff])
        rb4 = sch.add("v", lambda e: e.tensor_copy(
            out=rhs35[:, :, 3:3 + E],
            in_=ahib.rearrange("p (n e) -> p n e", e=E)[:, :, :]), deps=[ahi])
        ahf = sch.add("v", lambda e: e.tensor_copy(
            out=logit[:, :, :], in_=ahib.rearrange("p (n e) -> p n e", e=E)[:, :, :]),
            deps=[ahi, g1])
        rb5 = sch.add("v", lambda e: e.tensor_tensor(
            out=rhs35[:, :, 3 + E:3 + 2 * E],
            in0=esb.rearrange("p (n e) -> p n e", e=E)[:, :, :],
            in1=logit[:, :, :], op=TS.subtract), deps=[ahf])
        rbs = [rb1, rb2, rb3, rb4, rb5]
        prev_oh = [[], []]
        pend_tail = None
        pyb_rd = [[], []]
        odl = []
        for ei in range(EH):
            bA = 2 + 2 * (ei % 2)
            mm_last = [None] * NSL
            for ntk in range(NT):
                buf = ntk % 2
                oh = sch.add("v", lambda e, ei=ei, ntk=ntk, buf=buf: e.tensor_scalar(
                    out=ohb[:, buf, 0:CAP], in0=io256[:, 0:CAP],
                    scalar1=ctok[:, ntk, ei:ei + 1], scalar2=None, op0=TS.is_equal),
                    deps=[ct_ev[ntk], A["io256"]] + prev_oh[buf])
                prev_oh[buf] = []
                for sl in range(NSL):
                    w = min(128, CAP - 128 * sl)
                    dep0 = ([oh] + rbs + [ps_user[bA + sl]]) if ntk == 0 else [oh]
                    mm = sch.add(
                        "t", lambda e, ntk=ntk, sl=sl, w=w, buf=buf, bA=bA,
                        s=(ntk == 0), p=(ntk == NT - 1):
                        e.matmul(ps[bA + sl][0:NW, 0:w], rhs35[:, ntk, :],
                                 ohb[:, buf, 128 * sl:128 * sl + w], start=s, stop=p),
                        deps=dep0)
                    mm_last[sl] = mm
                    prev_oh[buf].append(mm)
            ev_sl = []
            for sl in range(NSL):
                w = min(128, CAP - 128 * sl)
                evc = sch.add("v", lambda e, ei=ei, sl=sl, w=w, bA=bA: e.tensor_copy(
                    out=ohsb[0:NW, ei % 2, 128 * sl:128 * sl + w],
                    in_=ps[bA + sl][0:NW, 0:w]), deps=[mm_last[sl]])
                ps_user[bA + sl] = evc
                ev_sl.append(evc)
            if pend_tail is not None:
                pend_tail()

            def pend_tail(ei=ei, ev_sl=ev_sl):
                for sl in range(NSL):
                    w = min(128, CAP - 128 * sl)
                    tp = sch.add("t", lambda e, ei=ei, sl=sl, w=w: e.transpose(
                        ps[6 + sl][0:w, 0:NW], ohsb[0:NW, ei % 2, 128 * sl:128 * sl + w],
                        identsb[0:NW, 0:NW]), deps=[ev_sl[sl], ps_user[6 + sl]])
                    evy = sch.add("v", lambda e, sl=sl, w=w: e.tensor_copy(
                        out=pyb[0:w, sl, :], in_=ps[6 + sl][0:w, 0:NW]),
                        deps=[tp] + pyb_rd[sl])
                    ps_user[6 + sl] = evy
                    f1 = sch.add("v", lambda e, sl=sl, w=w: e.tensor_scalar(
                        out=smallf[0:w, 0:1], in0=pyb[0:w, sl, 2:3], scalar1=-float(T),
                        scalar2=float(T), op0=TS.mult, op1=TS.add), deps=[evy])
                    f2 = sch.add("v", lambda e, sl=sl, w=w: e.tensor_scalar(
                        out=smallf[0:w, 1:2], in0=pyb[0:w, sl, 0:1], scalar1=32.0,
                        scalar2=None, op0=TS.mult), deps=[evy])
                    f3 = sch.add("v", lambda e, sl=sl, w=w: e.tensor_tensor(
                        out=smallf[0:w, 2:3], in0=smallf[0:w, 1:2],
                        in1=pyb[0:w, sl, 1:2], op=TS.add), deps=[f2])
                    f4 = sch.add("v", lambda e, w=w: e.tensor_tensor(
                        out=smallf[0:w, 3:4], in0=smallf[0:w, 2:3],
                        in1=smallf[0:w, 0:1], op=TS.add), deps=[f1, f3])
                    f5 = sch.add("v", lambda e, ei=ei, sl=sl, w=w: e.tensor_copy(
                        out=idx_sb[0:w, ei:ei + 1, sl:sl + 1],
                        in_=smallf[0:w, 3:4].unsqueeze(2)), deps=[f4])
                    fg = sch.add("v", lambda e, ei=ei, sl=sl, w=w: e.tensor_tensor(
                        out=gate_sb[0:w, ei:ei + 1, sl:sl + 1],
                        in0=pyb[0:w, sl, 3 + ei:4 + ei].unsqueeze(2),
                        in1=pyb[0:w, sl, 3 + E + ei:4 + E + ei].unsqueeze(2),
                        op=TS.add), deps=[evy])
                    pyb_rd[sl] = [f5, fg]
                    odl.append(sch.add("q", lambda e, sl=sl, w=w, ei=ei: e.dma_start(
                        out=idx_o[ei:ei + 1, 128 * sl:128 * sl + w].rearrange(
                            "a p -> p a").unsqueeze(2),
                        in_=idx_sb[0:w, ei:ei + 1, sl:sl + 1],
                        allow_slow_non_contiguous=True), deps=[f5]))
                    odl.append(sch.add("q", lambda e, sl=sl, w=w, ei=ei: e.dma_start(
                        out=gate_o[ei:ei + 1, 128 * sl:128 * sl + w].rearrange(
                            "a p -> p a").unsqueeze(2),
                        in_=gate_sb[0:w, ei:ei + 1, sl:sl + 1],
                        allow_slow_non_contiguous=True), deps=[fg]))
        pend_tail()
        return {"ln_dma": ln_dma, "odl": odl, "idx_sb": idx_sb,
                "gate_sb": gate_sb}


def build_l2(TTOT=16384, T=4096, D=1024, DFF=4096, CAP=256, NEX=2, NB=4):
    NSL = ceil_div(CAP, 128)
    NCELL = NEX * NB
    nc = bass.Bass("TRN2", target_bir_lowering=False, debug=False)
    io = {}
    io["xmid_all"] = nc.dram_tensor("xmid_all", [TTOT, D], BF16, kind="ExternalInput").ap()
    io["idx_i"] = nc.dram_tensor("idx_i", [NCELL, CAP], U32, kind="ExternalInput").ap()
    io["gate_i"] = nc.dram_tensor("gate_i", [NCELL, CAP], F32, kind="ExternalInput").ap()
    io["w1_i"] = nc.dram_tensor("w1_i", [NEX, D, DFF], F32, kind="ExternalInput").ap()
    io["w2_i"] = nc.dram_tensor("w2_i", [NEX, DFF, D], F32, kind="ExternalInput").ap()
    io["b1_i"] = nc.dram_tensor("b1_i", [NEX, DFF], F32, kind="ExternalInput").ap()
    io["b2_i"] = nc.dram_tensor("b2_i", [NEX, D], F32, kind="ExternalInput").ap()
    io["identb"] = nc.dram_tensor("identb", [128, 128], BF16, kind="ExternalInput").ap()
    io["eo_o"] = nc.dram_tensor("eo_o", [NCELL, NSL, 128, D], BF16, kind="ExternalOutput").ap()
    sch = Sched()
    with ExitStack() as ctx:
        sb = lambda name, shape, dt: ctx.enter_context(nc.sbuf_tensor(name, shape, dt))
        ps = mk_psum(nc, ctx)
        st = emit_l2(nc, sch, sb, ps, io, TTOT=TTOT, T=T, D=D, DFF=DFF, CAP=CAP,
                     NEX=NEX, NB=NB, base_stride=T)
        fin = sch.add("g", lambda e: e.nop(), inc=1, deps=st["all_eo"])
        sch.emit_all(nc, ctx)
    return nc


def emit_l2(nc, sch, sb, ps, io, TTOT=16384, T=4096, D=1024, DFF=4096, CAP=256,
            NEX=2, NB=4, base_stride=None, cell_base=None, chunked=None,
            predeps=()):
    DC = D // 128
    FC = DFF // 128
    NSL = ceil_div(CAP, 128)
    NCELL = NEX * NB
    if cell_base is None:
        cell_base = [b * base_stride for b in range(NB)]
    xmid_all = io["xmid_all"]
    idx_i = io.get("idx_i"); gate_i = io.get("gate_i")
    w1_i = io["w1_i"]; w2_i = io["w2_i"]; b1_i = io["b1_i"]; b2_i = io["b2_i"]
    identb = io["identb"]; eo_o = io["eo_o"]
    predeps = list(predeps)
    if True:
        w1b = sb("w1b", [128, DC, DFF], BF16)
        w2b = sb("w2b", [128, FC, D], BF16)
        b1b = sb("b1b", [128, FC], F32)
        b2b = sb("b2b", [1, NEX * D], BF16)
        idxs = sb("idxs", [128, NCELL, NSL], U32)
        idxf = sb("idxf", [128, NSL], F32)
        kbuf = sb("kbuf", [128, 2, NSL], F32)
        gts = sb("gts", [128, NCELL, NSL], F32)
        gath = sb("gath", [128, NSL, D], BF16)
        eiT = sb("eiT", [128, DC, CAP], BF16)
        hb = sb("hb", [128, FC, CAP], BF16)
        eob = sb("eob", [128, 2, 512], BF16)
        idb = sb("idb", [128, 128], BF16)
        onesr = sb("onesr", [1, 128], BF16)
        A = {}
        A["ident"] = sch.add("q", lambda e: e.dma_start(out=idb[:, :], in_=identb[:, :]),
                             deps=predeps)
        PW = min(128, CAP)
        idx_src = io.get("idx_pcs")
        if idx_src is None:
            idx_src = idx_i.rearrange("c (s p) -> p c s", p=PW)
        gate_src = io.get("gate_pcs")
        if gate_src is None:
            gate_src = gate_i.rearrange("c (s p) -> p c s", p=PW)
        idx_out = idxs[0:PW, :, :]
        gts_out = gts[0:PW, :, :]
        if len(idx_src.shape) == 4:
            idx_out = idx_out.rearrange("p (x bb) s -> p x bb s", x=NEX)
            gts_out = gts_out.rearrange("p (x bb) s -> p x bb s", x=NEX)
        A["idx"] = sch.add("q", lambda e: e.dma_start(
            out=idx_out, in_=idx_src, allow_slow_non_contiguous=True),
            deps=predeps)
        A["gts"] = sch.add("q", lambda e: e.dma_start(
            out=gts_out, in_=gate_src, allow_slow_non_contiguous=True),
            deps=predeps)
        A["onesr"] = sch.add("v", lambda e: e.memset(onesr[:, :], 1.0))

        ps_user = [None] * 8
        all_eo = []
        w_rd = [None, None]
        b_rd = [None, None]
        gath_rd = None
        eiT_rd = None
        hb_rd = None
        eob_rd = [None, None]
        idxf_rd = None

        for ex in range(NEX):
            wl1 = []
            wl2 = []
            for c in range(DC):
                wl1.append(sch.add("g", lambda e, ex=ex, c=c: e.dma_start(
                    out=w1b[:, c, :], in_=w1_i[ex, 128 * c:128 * c + 128, :]),
                    deps=(predeps + [w_rd[0]]) if c == 0 else []))
            for f in range(FC):
                wl2.append(sch.add("g", lambda e, ex=ex, f=f: e.dma_start(
                    out=w2b[:, f, :], in_=w2_i[ex, 128 * f:128 * f + 128, :]),
                    deps=[w_rd[1]] if f == 0 and w_rd[1] else []))
            bl = sch.add("q", lambda e, ex=ex: e.dma_start(
                out=b1b[:, :], in_=b1_i[ex:ex + 1, :].rearrange("a (t p) -> p (a t)", p=128),
                allow_slow_non_contiguous=True),
                deps=[b_rd[0]] if b_rd[0] else [])
            b2l = sch.add("g", lambda e, ex=ex: e.dma_start(
                out=b2b[:, ex * D:(ex + 1) * D], in_=b2_i[ex:ex + 1, :]))
            for b in range(NB):
                cell = ex * NB + b
                i1 = sch.add("v", lambda e, cell=cell: e.tensor_copy(
                    out=idxf[0:PW, :], in_=idxs[0:PW, cell, :]),
                    deps=[A["idx"]] + ([idxf_rd] if idxf_rd else []))
                if chunked is not None and chunked[1] > 1:
                    RPC, NCH = chunked
                    prevk = i1
                    for j in range(1, NCH):
                        op0 = sch.add("v", lambda e, j=j: e.tensor_scalar(
                            out=kbuf[0:PW, 1, :], in0=idxf[0:PW, :],
                            scalar1=float(j * RPC) - 0.5, scalar2=None,
                            op0=TS.is_gt), deps=[prevk])
                        if j == 1:
                            prevk = sch.add("v", lambda e: e.tensor_copy(
                                out=kbuf[0:PW, 0, :], in_=kbuf[0:PW, 1, :]),
                                deps=[op0])
                        else:
                            prevk = sch.add("v", lambda e: e.tensor_add(
                                kbuf[0:PW, 0, :], kbuf[0:PW, 0, :], kbuf[0:PW, 1, :]),
                                deps=[op0, prevk])
                    ksc = sch.add("v", lambda e: e.tensor_scalar(
                        out=kbuf[0:PW, 0, :], in0=kbuf[0:PW, 0, :],
                        scalar1=float((NCORES - 1) * RPC), scalar2=None,
                        op0=TS.mult), deps=[prevk])
                    i2 = sch.add("v", lambda e, b=b: e.tensor_scalar(
                        out=idxf[0:PW, :], in0=idxf[0:PW, :],
                        scalar1=float(cell_base[b]), scalar2=None, op0=TS.add),
                        deps=[i1, ksc])
                    i2 = sch.add("v", lambda e: e.tensor_add(
                        idxf[0:PW, :], idxf[0:PW, :], kbuf[0:PW, 0, :]),
                        deps=[i2])
                else:
                    i2 = sch.add("v", lambda e, b=b: e.tensor_scalar(
                        out=idxf[0:PW, :], in0=idxf[0:PW, :],
                        scalar1=float(cell_base[b]), scalar2=None, op0=TS.add),
                        deps=[i1])
                i3 = sch.add("v", lambda e, cell=cell: e.tensor_copy(
                    out=idxs[0:PW, cell, :], in_=idxf[0:PW, :]), deps=[i2])
                g_ops = []
                for sl in range(NSL):
                    wj = min(128, CAP - 128 * sl)
                    g_ops.append(sch.add("g", lambda e, cell=cell, sl=sl, wj=wj: e.indirect_dma_start(
                        out=gath[0:wj, sl, :], out_offset=None, in_=xmid_all[:, :],
                        in_offset=bass.IndirectOffsetOnAxis(idxs[0:wj, cell, sl:sl + 1], 0)),
                        deps=[i3] + ([gath_rd] if gath_rd else []), ind=True))
                tp_ev = []
                for sl in range(NSL):
                    wj = min(128, CAP - 128 * sl)
                    for c in range(DC):
                        tp = sch.add("t", lambda e, sl=sl, c=c, wj=wj: e.transpose(
                            ps[0][0:128, 0:128].bitcast(BF16)[:, 0:wj], gath[0:wj, sl, 128 * c:128 * c + 128],
                            idb[0:wj, 0:wj]), deps=[g_ops[sl], A["ident"], ps_user[0]])
                        ev = sch.add("v", lambda e, sl=sl, c=c, wj=wj: e.tensor_copy(
                            out=eiT[:, c, 128 * sl:128 * sl + wj],
                            in_=ps[0][0:128, 0:128].bitcast(BF16)[:, 0:wj]),
                            deps=[tp] + ([eiT_rd] if eiT_rd and sl == 0 and c == 0 else []))
                        ps_user[0] = ev
                        tp_ev.append(ev)
                h_ev = []
                for ft in range(FC):
                    pb = 1 + ft % 2
                    acc = None
                    for c in range(DC):
                        acc = sch.add("t", lambda e, ft=ft, c=c, pb=pb, s=(c == 0), p=(c == DC - 1):
                                      e.matmul(ps[pb][0:128, 0:CAP], w1b[:, c, 128 * ft:128 * ft + 128],
                                               eiT[:, c, :], start=s, stop=p),
                                      deps=(tp_ev + [wl1[0], ps_user[pb]]) if c == 0
                                      else [wl1[c]])
                    hg = sch.add("s", lambda e, ft=ft, pb=pb: e.activation(
                        out=hb[:, ft, :], in_=ps[pb][0:128, 0:CAP], func=AF.Gelu_apprx_tanh,
                        bias=b1b[:, ft:ft + 1], scale=1.0),
                        deps=[acc, bl] + ([hb_rd] if hb_rd and ft == 0 else []))
                    ps_user[pb] = hg
                    h_ev.append(hg)
                eo_dma = []
                for js in range(NSL):
                    wj = min(128, CAP - 128 * js)
                    for dh in range(ceil_div(D, 512)):
                        pb = 3 + (2 * js + dh) % 2
                        acc = None
                        for ft in range(FC):
                            wd = min(512, D - 512 * dh)
                            acc = sch.add("t", lambda e, js=js, dh=dh, ft=ft, pb=pb, wj=wj, wd=wd,
                                          s=(ft == 0):
                                          e.matmul(ps[pb][0:wj, 0:wd],
                                                   hb[:, ft, 128 * js:128 * js + wj],
                                                   w2b[:, ft, 512 * dh:512 * dh + wd],
                                                   start=s, stop=False),
                                          deps=(h_ev + [wl2[0], ps_user[pb]]) if ft == 0
                                          else [wl2[ft]])
                        wd = min(512, D - 512 * dh)
                        acc = sch.add("t", lambda e, ex=ex, dh=dh, pb=pb, wj=wj, wd=wd: e.matmul(
                            ps[pb][0:wj, 0:wd], onesr[:, 0:wj],
                            b2b[:, ex * D + 512 * dh:ex * D + 512 * dh + wd],
                            start=False, stop=True),
                            deps=[A["onesr"], b2l])
                        gm = sch.add("v", lambda e, cell=cell, js=js, dh=dh, pb=pb, wj=wj, wd=wd: e.tensor_scalar(
                            out=eob[0:wj, dh % 2, 0:wd], in0=ps[pb][0:wj, 0:wd],
                            scalar1=gts[0:wj, cell, js:js + 1], scalar2=None, op0=TS.mult),
                            deps=[acc, A["gts"]] + ([eob_rd[dh % 2]] if eob_rd[dh % 2] else []))
                        ps_user[pb] = gm
                        dma = sch.add("q", lambda e, cell=cell, js=js, dh=dh, wj=wj, wd=wd: e.dma_start(
                            out=eo_o[cell, js, 0:wj, 512 * dh:512 * dh + wd], in_=eob[0:wj, dh % 2, 0:wd]),
                            deps=[gm])
                        eob_rd[dh % 2] = dma
                        eo_dma.append(dma)
                        all_eo.append(dma)
                gath_rd = tp_ev[-1]
                eiT_rd = h_ev[-1] if h_ev else None
                hb_rd = eo_dma[-1]
                idxf_rd = i3
            w_rd = [acc, acc]
            b_rd = [bl, b2l]
        return {"all_eo": all_eo}


def build_l3(T=4096, D=1024, CAP=256, E=16):
    NSL = ceil_div(CAP, 128)
    nc = bass.Bass("TRN2", target_bir_lowering=False, debug=False)
    io = {}
    io["eo_i"] = nc.dram_tensor("eo_i", [E, NSL, 128, D], BF16, kind="ExternalInput").ap()
    io["idx_i"] = nc.dram_tensor("idx_i", [E, CAP], U32, kind="ExternalInput").ap()
    io["xmid_i"] = nc.dram_tensor("xmid_i", [T, D], F32, kind="ExternalInput").ap()
    io["out_o"] = nc.dram_tensor("out_o", [T, D], F32, kind="ExternalOutput").ap()
    io["accum"] = [nc.dram_tensor("accum0", [T + 128, D], F32).ap(),
                   nc.dram_tensor("accum1", [T + 128, D], F32).ap()]
    sch = Sched()
    with ExitStack() as ctx:
        sb = lambda name, shape, dt: ctx.enter_context(nc.sbuf_tensor(name, shape, dt))
        st = emit_l3(nc, sch, sb, io, T=T, D=D, CAP=CAP, E=E)
        fin = sch.add("g", lambda e: e.nop(), inc=1, deps=st["out_dma"])
        sch.emit_all(nc, ctx)
    return nc


def emit_l3(nc, sch, sb, io, T=4096, D=1024, CAP=256, E=16, predeps=(),
            idx_sb_ext=None, ln_split=False):
    NSL = ceil_div(CAP, 128)
    NT = T // 128
    eo_i = io.get("eo_i"); xmid_i = io["xmid_i"]; out_o = io["out_o"]; accum = io["accum"]
    predeps = list(predeps)
    if True:
        NBUF = 6
        zt = sb("zt", [128, D], F32)
        idxs = sb("idxs", [128, E, NSL], U32)
        eosb = sb("eosb", [128, NBUF, D], BF16)
        acc_sb = sb("acc_sb", [128, 2, 2, D], F32)
        sqs = sb("sqs", [128, 2, D], F32)
        stats = sb("stats", [128, 16], F32)
        orow = sb("orow", [128, 2, D], F32)

        A = {}
        A["z"] = sch.add("v", lambda e: e.memset(zt[:, :], 0.0), deps=predeps)
        PW = min(128, CAP)
        if idx_sb_ext is None:
            idx_i = io["idx_i"]
            A["idx"] = sch.add("q", lambda e: e.dma_start(
                out=idxs[0:PW, :, :],
                in_=idx_i.rearrange("c (s p) -> p c s", p=PW),
                allow_slow_non_contiguous=True), deps=predeps)
        else:
            idxs = idx_sb_ext
            A["idx"] = None
        xm_deps = io.get("xmid_wr") or []
        zd = [[], []]
        for nt in range(NT):
            zd[0].append(sch.add("q", lambda e, nt=nt: e.dma_start(
                out=accum[0][128 * nt:128 * nt + 128, :],
                in_=xmid_i[128 * nt:128 * nt + 128, :]),
                deps=([xm_deps[nt]] if xm_deps else predeps)))
        zd[0].append(sch.add("q", lambda e: e.dma_start(
            out=accum[0][128 * NT:128 * NT + 128, :], in_=zt[:, :]), deps=[A["z"]]))
        for nt in range(NT + 1):
            zd[1].append(sch.add("q", lambda e, nt=nt: e.dma_start(
                out=accum[1][128 * nt:128 * nt + 128, :], in_=zt[:, :]),
                deps=[A["z"]]))
        eo_rows = io.get("eo_rows")
        eosel_sb = io.get("eosel_sb")
        last_sc = [None, None]
        eol = [None] * NBUF
        for ei in range(E):
            for sl in range(NSL):
                wj = min(128, CAP - 128 * sl)
                k = ei * NSL + sl
                par = k % 2
                buf = k % NBUF
                if eo_rows is None:
                    ld = sch.add("q", lambda e, ei=ei, sl=sl, buf=buf, wj=wj: e.dma_start(
                        out=eosb[0:wj, buf, :], in_=eo_i[ei, sl, 0:wj, :]),
                        deps=[eol[buf]] if eol[buf] else [])
                else:
                    ld = sch.add("g", lambda e, ei=ei, sl=sl, buf=buf, wj=wj: e.indirect_dma_start(
                        out=eosb[0:wj, buf, :], out_offset=None,
                        in_=eo_rows,
                        in_offset=bass.IndirectOffsetOnAxis(eosel_sb[0:wj, ei, sl:sl + 1], 0)),
                        deps=predeps + ([eol[buf]] if eol[buf] else []))
                sc = sch.add("g", lambda e, ei=ei, sl=sl, par=par, buf=buf, wj=wj:
                             e.indirect_dma_start(
                    out=accum[par][:, :], out_offset=bass.IndirectOffsetOnAxis(
                        idxs[0:wj, ei, sl:sl + 1], 0),
                    in_=eosb[0:wj, buf, :], in_offset=None,
                    compute_op=TS.add),
                    deps=[ld, A["idx"]] + zd[par], ind=1 + par)
                eol[buf] = sc
                last_sc[par] = sc
        out_dma = []
        stats_rd = [None, None]
        sq_rd = [None, None]
        buf_rd = [None, None]
        lnsel_sb = io.get("lnsel_sb")
        NLN = NT // 2 if ln_split else NT
        for nt in range(NLN):
            bb = nt % 2
            if ln_split:
                l1 = sch.add("g", lambda e, nt=nt, bb=bb: e.indirect_dma_start(
                    out=acc_sb[:, bb, 0, :], out_offset=None, in_=accum[0][:, :],
                    in_offset=bass.IndirectOffsetOnAxis(lnsel_sb[:, nt:nt + 1], 0)),
                    deps=[last_sc[0]] + ([buf_rd[bb]] if buf_rd[bb] else []))
                l1b = sch.add("g", lambda e, nt=nt, bb=bb: e.indirect_dma_start(
                    out=acc_sb[:, bb, 1, :], out_offset=None, in_=accum[1][:, :],
                    in_offset=bass.IndirectOffsetOnAxis(lnsel_sb[:, nt:nt + 1], 0)),
                    deps=[last_sc[1]] + ([buf_rd[bb]] if buf_rd[bb] else []))
            else:
                l1 = sch.add("q", lambda e, nt=nt, bb=bb: e.dma_start(
                    out=acc_sb[:, bb, 0, :], in_=accum[0][128 * nt:128 * nt + 128, :]),
                    deps=[last_sc[0]] + ([buf_rd[bb]] if buf_rd[bb] else []))
                l1b = sch.add("q", lambda e, nt=nt, bb=bb: e.dma_start(
                    out=acc_sb[:, bb, 1, :], in_=accum[1][128 * nt:128 * nt + 128, :]),
                    deps=[last_sc[1]] + ([buf_rd[bb]] if buf_rd[bb] else []))
            r1 = sch.add("v", lambda e, bb=bb: e.tensor_add(
                orow[:, bb, :], acc_sb[:, bb, 0, :], acc_sb[:, bb, 1, :]),
                deps=[l1, l1b] + ([buf_rd[bb]] if buf_rd[bb] else []))
            so = 8 * bb
            c1 = sch.add("v", lambda e, bb=bb, so=so: e.reduce_sum(
                stats[:, so + 0:so + 1], orow[:, bb, :], axis=AX.X),
                deps=[r1] + ([stats_rd[bb]] if stats_rd[bb] else []))
            c2 = sch.add("v", lambda e, so=so: e.tensor_scalar(
                out=stats[:, so + 0:so + 1], in0=stats[:, so + 0:so + 1],
                scalar1=1.0 / D, scalar2=None, op0=TS.mult), deps=[c1])
            sqv = sch.add("s", lambda e, bb=bb: e.activation(
                out=sqs[:, bb, :], in_=orow[:, bb, :], func=AF.Square),
                deps=[r1] + ([sq_rd[bb]] if sq_rd[bb] else []))
            c4 = sch.add("v", lambda e, bb=bb, so=so: e.reduce_sum(
                stats[:, so + 1:so + 2], sqs[:, bb, :], axis=AX.X), deps=[sqv])
            c5 = sch.add("v", lambda e, so=so: e.tensor_scalar(
                out=stats[:, so + 1:so + 2], in0=stats[:, so + 1:so + 2],
                scalar1=1.0 / D, scalar2=None, op0=TS.mult), deps=[c4])
            c6 = sch.add("v", lambda e, so=so: e.tensor_mul(
                stats[:, so + 2:so + 3], stats[:, so + 0:so + 1],
                stats[:, so + 0:so + 1]), deps=[c2])
            c7 = sch.add("v", lambda e, so=so: e.tensor_tensor(
                out=stats[:, so + 1:so + 2], in0=stats[:, so + 1:so + 2],
                in1=stats[:, so + 2:so + 3], op=TS.subtract), deps=[c5, c6])
            c8 = sch.add("v", lambda e, so=so: e.tensor_scalar(
                out=stats[:, so + 1:so + 2], in0=stats[:, so + 1:so + 2],
                scalar1=1e-5, scalar2=None, op0=TS.add), deps=[c7])
            c9 = sch.add("s", lambda e, so=so: e.activation(
                out=stats[:, so + 3:so + 4], in_=stats[:, so + 1:so + 2],
                func=AF.Sqrt), deps=[c8])
            c10 = sch.add("v", lambda e, so=so: e.reciprocal(
                stats[:, so + 4:so + 5], stats[:, so + 3:so + 4]), deps=[c9])
            xm = sch.add("v", lambda e, bb=bb, so=so: e.tensor_scalar(
                out=orow[:, bb, :], in0=orow[:, bb, :], scalar1=stats[:, so + 0:so + 1],
                scalar2=stats[:, so + 4:so + 5], op0=TS.subtract, op1=TS.mult),
                deps=[c10])
            sq_rd[bb] = c4
            if ln_split:
                d1 = sch.add("g", lambda e, nt=nt, bb=bb: e.indirect_dma_start(
                    out=out_o[:, :], out_offset=bass.IndirectOffsetOnAxis(
                        lnsel_sb[:, nt:nt + 1], 0),
                    in_=orow[:, bb, :], in_offset=None), deps=[xm])
            else:
                d1 = sch.add("q", lambda e, nt=nt, bb=bb: e.dma_start(
                    out=out_o[128 * nt:128 * nt + 128, :], in_=orow[:, bb, :]),
                    deps=[xm])
            out_dma.append(d1)
            buf_rd[bb] = d1
            stats_rd[bb] = xm
        return {"out_dma": out_dma}


_T, _D, _H, _E, _CAP, _DFF, _B = 4096, 1024, 16, 16, 256, 4096, 4


def _consts(T, D):
    import ml_dtypes
    NT = T // 128
    masks = np.zeros((128, 4, 512), np.float32)
    ar = np.arange(512)
    for r in range(4):
        for p in range(128):
            masks[p, r, :] = (128 * r + p <= ar)
    return {
        "identf": np.eye(128, dtype=np.float32),
        "masks": masks.astype(ml_dtypes.bfloat16),
        "iota256": np.broadcast_to(np.arange(256, dtype=np.float32) + 1.0,
                                   (128, 256)).copy(),
        "iotaN": (128 * np.arange(NT, dtype=np.float32)[None, :]
                  + np.arange(128, dtype=np.float32)[:, None]).copy(),
        "iotaNhi": np.asarray((128 * np.arange(NT)[None, :]
                   + np.arange(128)[:, None]) // 32, dtype=ml_dtypes.bfloat16),
        "iotaNlo": np.asarray((128 * np.arange(NT)[None, :]
                   + np.arange(128)[:, None]) % 32, dtype=ml_dtypes.bfloat16),
        "ones_in": np.ones((128, 128), np.float32),
    }


def kernel(**inputs):
    return kernel_fused(**inputs)


def kernel_v0_multilaunch(L0_attn_x, L0_attn_W_qkv, L0_attn_W_proj, L0_ffn_gate_query,
           L0_ffn_log_temp, L0_ffn_w1, L0_ffn_b1, L0_ffn_w2, L0_ffn_b2):
    import ml_dtypes
    T, D, H, E, CAP, DFF, B = _T, _D, _H, _E, _CAP, _DFF, _B
    x = np.ascontiguousarray(np.asarray(L0_attn_x, dtype=np.float32))
    wqkv = np.asarray(L0_attn_W_qkv, dtype=np.float32)
    wproj = np.ascontiguousarray(np.asarray(L0_attn_W_proj, dtype=np.float32))
    gq = np.asarray(L0_ffn_gate_query, dtype=np.float32)
    lt = np.asarray(L0_ffn_log_temp, dtype=np.float32).reshape(1, 1)
    w1 = np.asarray(L0_ffn_w1, dtype=np.float32)
    b1 = np.asarray(L0_ffn_b1, dtype=np.float32)
    w2 = np.asarray(L0_ffn_w2, dtype=np.float32)
    b2 = np.asarray(L0_ffn_b2, dtype=np.float32)

    HP = H // 2
    wq_p = np.empty((HP, D, 384), np.float32)
    for hp in range(HP):
        wq_p[hp, :, 0:128] = wqkv[:, 128 * hp:128 * hp + 128]
        wq_p[hp, :, 128:256] = wqkv[:, D + 128 * hp:D + 128 * hp + 128]
        wq_p[hp, :, 256:384] = wqkv[:, 2 * D + 128 * hp:2 * D + 128 * hp + 128]
    cst = _consts(T, D)
    gqT = np.ascontiguousarray(gq.T)

    nc1 = build_l1(T=T, D=D, H=H, E=E, CAP=CAP)
    in1 = []
    for c in range(NCORES):
        b = c // 2
        in1.append(dict(xT=np.ascontiguousarray(x[b].T), wqkv_p=wq_p, wproj=wproj,
                        gqT=gqT, ltemp=lt, **cst))
    r1 = run_bass_kernel_spmd(nc1, in1, core_ids=list(range(NCORES))).results

    xmid_all = np.concatenate([np.asarray(r1[2 * b]["xmid_b"]) for b in range(B)],
                              axis=0)
    identb = np.eye(128, dtype=ml_dtypes.bfloat16)
    nc2 = build_l2(TTOT=B * T, T=T, D=D, DFF=DFF, CAP=CAP, NEX=2, NB=B)
    in2 = []
    for c in range(NCORES):
        exs = [2 * c, 2 * c + 1]
        idx_i = np.empty((2 * B, CAP), np.uint32)
        gate_i = np.empty((2 * B, CAP), np.float32)
        for xi, ex in enumerate(exs):
            for b in range(B):
                idx_i[xi * B + b] = np.asarray(r1[2 * b]["idx_o"])[ex]
                gate_i[xi * B + b] = np.asarray(r1[2 * b]["gate_o"])[ex]
        in2.append(dict(xmid_all=xmid_all, idx_i=idx_i, gate_i=gate_i,
                        w1_i=np.ascontiguousarray(w1[exs]),
                        w2_i=np.ascontiguousarray(w2[exs]),
                        b1_i=np.ascontiguousarray(b1[exs]),
                        b2_i=np.ascontiguousarray(b2[exs]), identb=identb))
    r2 = run_bass_kernel_spmd(nc2, in2, core_ids=list(range(NCORES))).results

    NSL = ceil_div(CAP, 128)
    nc3 = build_l3(T=T, D=D, CAP=CAP, E=E)
    in3 = []
    for c in range(NCORES):
        b = c // 2
        eo_i = np.empty((E, NSL, 128, D), ml_dtypes.bfloat16)
        for e in range(E):
            eo_i[e] = np.asarray(r2[e // 2]["eo_o"]).reshape(2 * B, NSL, 128, D)[
                (e % 2) * B + b]
        in3.append(dict(eo_i=eo_i, idx_i=np.asarray(r1[2 * b]["idx_o"]),
                        xmid_i=np.asarray(r1[2 * b]["xmid_f"])))
    r3 = run_bass_kernel_spmd(nc3, in3, core_ids=list(range(NCORES))).results

    out = np.stack([np.asarray(r3[2 * b]["out_o"]) for b in range(B)], axis=0)
    return out.astype(np.float32)


def build_fused(T=4096, D=1024, H=16, E=16, CAP=256, DFF=4096, NB=4,
                BIS_ITERS=36, SINK_ITERS=6):
    HP = H // 2
    NT = T // 128
    NSL = ceil_div(CAP, 128)
    NEX = E // NCORES
    NCELL = NEX * NB
    PW = min(128, CAP)
    nc = bass.Bass("TRN2", target_bir_lowering=False, debug=False)
    io1 = {}
    io1["xT"] = nc.dram_tensor("xT", [D, T], F32, kind="ExternalInput").ap()
    HPL = HP // 2
    NJ = T // 512
    io1["wqkv_p"] = nc.dram_tensor("wqkv_p", [HPL, D, 3 * 128], F32, kind="ExternalInput").ap()
    io1["wproj"] = nc.dram_tensor("wproj", [D, D], F32, kind="ExternalInput").ap()
    io1["gqT"] = nc.dram_tensor("gqT", [D, E], F32, kind="ExternalInput").ap()
    io1["ltemp"] = nc.dram_tensor("ltemp", [1, 1], F32, kind="ExternalInput").ap()
    io1["identf"] = nc.dram_tensor("identf", [128, 128], F32, kind="ExternalInput").ap()
    io1["masks"] = nc.dram_tensor("masks", [128, 4, 512], BF16, kind="ExternalInput").ap()
    io1["iota256"] = nc.dram_tensor("iota256", [128, 256], F32, kind="ExternalInput").ap()
    io1["iotaN"] = nc.dram_tensor("iotaN", [128, NT], F32, kind="ExternalInput").ap()
    io1["iotaNhi"] = nc.dram_tensor("iotaNhi", [128, NT], BF16, kind="ExternalInput").ap()
    io1["iotaNlo"] = nc.dram_tensor("iotaNlo", [128, NT], BF16, kind="ExternalInput").ap()
    io1["ones_in"] = nc.dram_tensor("ones_in", [128, 128], F32, kind="ExternalInput").ap()
    identb = nc.dram_tensor("identb", [128, 128], BF16, kind="ExternalInput").ap()
    w1_i = nc.dram_tensor("w1_i", [NEX, D, DFF], F32, kind="ExternalInput").ap()
    w2_i = nc.dram_tensor("w2_i", [NEX, DFF, D], F32, kind="ExternalInput").ap()
    b1_i = nc.dram_tensor("b1_i", [NEX, DFF], F32, kind="ExternalInput").ap()
    b2_i = nc.dram_tensor("b2_i", [NEX, D], F32, kind="ExternalInput").ap()
    out_o = nc.dram_tensor("out_o", [T, D], F32, kind="ExternalOutput").ap()
    oacc_t = [nc.dram_tensor(f"oacc{l}_h", [128, NJ, 512], BF16) for l in range(HPL)]
    io1["oacc_h"] = [t.ap() for t in oacc_t]
    oall_t = [nc.dram_tensor(f"oall{l}_h", [NCORES, 128, NJ, 512], BF16)
              for l in range(HPL)]
    oall_rows = [t.ap().rearrange("s p j f -> (s p j) f") for t in oall_t]
    io1["osel"] = nc.dram_tensor("osel", [128, D // 128, NJ], U32,
                                 kind="ExternalInput").ap()
    xmid_f_h = nc.dram_tensor("xmid_f_h", [T, D], F32)
    xmid_b_h = nc.dram_tensor("xmid_b_h", [T, D], BF16)
    xmid_all_h = nc.dram_tensor("xmid_all_h", [NCORES * T, D], BF16)
    rall_send = nc.dram_tensor("rall_send", [E // 2, CAP, 2], U32)
    rrall_h = nc.dram_tensor("rrall_h", [NCORES, E // 2, CAP, 2], U32)
    rg_h = nc.dram_tensor("rg_h", [NCELL, CAP, 2], U32)
    eoloc_h = nc.dram_tensor("eoloc_h", [NCELL, NSL, 128, D], BF16)
    eoall_h = nc.dram_tensor("eoall_h", [NCORES, NCELL, NSL, 128, D], BF16)
    rsel = nc.dram_tensor("rsel", [NCELL, 1], U32, kind="ExternalInput")
    p3rsel = nc.dram_tensor("p3rsel", [E, 1], U32, kind="ExternalInput")
    lnsel = nc.dram_tensor("lnsel", [128, NT // 2], U32, kind="ExternalInput")
    rg3_h = nc.dram_tensor("rg3_h", [E, CAP, 2], U32)
    eosel = nc.dram_tensor("eosel", [128, E, NSL], U32, kind="ExternalInput")
    accum_h = [nc.dram_tensor("accum0_h", [T + 128, D], F32),
               nc.dram_tensor("accum1_h", [T + 128, D], F32)]
    io1["xmid_f"] = xmid_f_h.ap()
    io1["xmid_b"] = xmid_b_h.ap()

    sch = Sched()
    with ExitStack() as ctx:
        persist = {}

        def psb(name, shape, dt):
            persist[name] = ctx.enter_context(nc.sbuf_tensor(name, shape, dt))
            return persist[name]

        arena = Arena(nc, ctx, 191 * 1024)
        PERSIST = {"idx_sb", "gate_sb"}

        def mk_sb():
            def sbf(name, shape, dt):
                if name in PERSIST:
                    return psb(name, shape, dt)
                return arena.alloc(shape, dt)
            return sbf

        ps = mk_psum(nc, ctx)

        io1["idx_o"] = nc.dram_tensor("idx_oh", [E, CAP], U32).ap()
        io1["gate_o"] = nc.dram_tensor("gate_oh", [E, CAP], F32).ap()

        sch.scope = "p1_attn"

        def oag(l, deps):
            sv = sch.scope
            sch.scope = "cc_oexch"
            op = sch.add("g", lambda e, l=l: e.collective_compute(
                "AllGather", TS.bypass, replica_groups=[list(range(NCORES))],
                ins=[oacc_t[l].ap().opt()], outs=[oall_t[l].ap().opt()]),
                deps=deps, ind=True, inc=1)
            sch.scope = sv
            return op

        arena.push()
        EH = E // 2
        st1 = emit_l1(nc, sch, mk_sb(), ps, io1, T=T, D=D, H=H, E=E, CAP=CAP,
                      BIS_ITERS=BIS_ITERS, SINK_ITERS=SINK_ITERS,
                      HPL=HPL, oag=oag, oall_rows=oall_rows, sel_experts=EH)
        arena.pop()
        idx_sb = persist["idx_sb"]
        gate_sb = persist["gate_sb"]

        sch.scope = "cc_route"
        pk = []
        pk.append(sch.add("q", lambda e: e.dma_start(
            out=rall_send.ap()[:, :, 0:1].rearrange("c (s p) a -> p c s a", p=PW),
            in_=idx_sb[0:PW, 0:EH, :].unsqueeze(3),
            allow_slow_non_contiguous=True), deps=st1["odl"]))
        pk.append(sch.add("q", lambda e: e.dma_start(
            out=rall_send.ap()[:, :, 1:2].rearrange("c (s p) a -> p c s a", p=PW),
            in_=gate_sb[0:PW, 0:EH, :].unsqueeze(3).bitcast(U32),
            allow_slow_non_contiguous=True), deps=st1["odl"]))
        NCH = 1
        TPC = NT // NCH
        RPC = T // NCH
        cc2l = []
        for k in range(NCH):
            cc2l.append(sch.add("g", lambda e, k=k: e.collective_compute(
                "AllGather", TS.bypass, replica_groups=[list(range(NCORES))],
                ins=[xmid_b_h.ap()[RPC * k:RPC * (k + 1), :].opt()],
                outs=[xmid_all_h.ap()
                      [NCORES * RPC * k:NCORES * RPC * (k + 1), :].opt()]),
                deps=[st1["ln_dma"][2 * i + 1] for i in range(TPC * k, TPC * (k + 1))],
                ind=True, inc=1))
        cc2 = cc2l[-1]
        cc1 = sch.add("g", lambda e: e.collective_compute(
            "AllGather", TS.bypass, replica_groups=[list(range(NCORES))],
            ins=[rall_send.ap().opt()], outs=[rrall_h.ap().opt()]),
            deps=pk, ind=True, inc=1)
        rsel_sb = psb("rsel_sb", [NCELL, 1], U32)
        rg_sb = psb("rg_sb", [NCELL, CAP * 2], U32)
        ld_rsel = sch.add("q", lambda e: e.dma_start(out=rsel_sb[:, :], in_=rsel.ap()))
        g_rg = sch.add("g", lambda e: e.indirect_dma_start(
            out=rg_sb[:, :], out_offset=None,
            in_=rrall_h.ap().rearrange("s c w f -> (s c) (w f)"),
            in_offset=bass.IndirectOffsetOnAxis(rsel_sb[:, 0:1], 0)),
            deps=[ld_rsel, cc1], ind=True)
        st_rg = sch.add("q", lambda e: e.dma_start(out=rg_h.ap().rearrange("c w f -> c (w f)"),
                                                   in_=rg_sb[:, :]), deps=[g_rg])
        p3rsel_sb = psb("p3rsel_sb", [E, 1], U32)
        rg3_sb = psb("rg3_sb", [E, CAP * 2], U32)
        ld_p3r = sch.add("q", lambda e: e.dma_start(out=p3rsel_sb[:, :],
                                                    in_=p3rsel.ap()))
        g_rg3 = sch.add("g", lambda e: e.indirect_dma_start(
            out=rg3_sb[:, :], out_offset=None,
            in_=rrall_h.ap().rearrange("s c w f -> (s c) (w f)"),
            in_offset=bass.IndirectOffsetOnAxis(p3rsel_sb[:, 0:1], 0)),
            deps=[ld_p3r, cc1])
        st_rg3 = sch.add("q", lambda e: e.dma_start(
            out=rg3_h.ap().rearrange("c w f -> c (w f)"), in_=rg3_sb[:, :]),
            deps=[g_rg3])

        sch.scope = "p2_ffn"
        bar1 = sch.barrier()
        idx_pcs = rg_h.ap().rearrange("c (s p) f -> p c s f", p=PW)[:, :, :, 0]
        gate_pcs = rg_h.ap().rearrange("c (s p) f -> p c s f", p=PW)[:, :, :, 1].bitcast(F32)
        io2 = {"xmid_all": xmid_all_h.ap(), "idx_pcs": idx_pcs,
               "gate_pcs": gate_pcs, "w1_i": w1_i, "w2_i": w2_i,
               "b1_i": b1_i, "b2_i": b2_i, "identb": identb,
               "eo_o": eoloc_h.ap()}
        arena.push()
        st2 = emit_l2(nc, sch, mk_sb(), ps, io2, TTOT=NCORES * T, T=T, D=D,
                      DFF=DFF, CAP=CAP, NEX=NEX, NB=NB,
                      cell_base=[2 * b * (T // NCH) for b in range(NB)],
                      chunked=(T // NCH, NCH),
                      predeps=bar1 + [st_rg, cc2])
        arena.pop()

        sch.scope = "cc_eo"
        cc3 = sch.add("g", lambda e: e.collective_compute(
            "AllGather", TS.bypass, replica_groups=[list(range(NCORES))],
            ins=[eoloc_h.ap().opt()], outs=[eoall_h.ap().opt()]),
            deps=st2["all_eo"], ind=True, inc=1)
        eosel_sb = psb("eosel_sb", [128, E, NSL], U32)
        ld_eosel = sch.add("q", lambda e: e.dma_start(out=eosel_sb[:, :, :],
                                                      in_=eosel.ap()))
        lnsel_sb = psb("lnsel_sb", [128, NT // 2], U32)
        ld_lnsel = sch.add("q", lambda e: e.dma_start(out=lnsel_sb[:, :],
                                                      in_=lnsel.ap()))

        sch.scope = "p3_scatter"
        bar2 = sch.barrier()
        io3 = {"eo_i": None,
               "eo_rows": eoall_h.ap().rearrange("s c n p d -> (s c n p) d"),
               "eosel_sb": eosel_sb, "idx_i": rg3_h.ap()[:, :, 0],
               "lnsel_sb": lnsel_sb,
               "xmid_i": xmid_f_h.ap(), "out_o": out_o,
               "accum": [a.ap() for a in accum_h],
               "xmid_wr": [st1["ln_dma"][2 * i] for i in range(NT)]}
        arena.push()
        st3 = emit_l3(nc, sch, mk_sb(), io3, T=T, D=D, CAP=CAP, E=E,
                      predeps=bar2 + [cc3, ld_eosel, st_rg3, ld_lnsel],
                      idx_sb_ext=None, ln_split=True)
        arena.pop()
        fin = sch.add("g", lambda e: e.nop(), inc=1, deps=st3["out_dma"])
        sch.emit_all(nc, ctx)
    return nc


def _osel_table(c, T, D):
    DC = D // 128
    HPL = DC // 2
    NJ = T // 512
    b = c // 2
    osel = np.zeros((128, DC, NJ), np.uint32)
    p = np.arange(128, dtype=np.uint32)
    for cc in range(DC):
        src = 2 * b + cc // HPL
        for j in range(NJ):
            osel[:, cc, j] = (src * 128 + p) * NJ + j
    return osel


def _sel_tables(c, T, E, CAP, NB):
    NEX = E // NCORES
    NCELL = NEX * NB
    NSL = ceil_div(CAP, 128)
    bc = c // 2
    EH = E // 2
    rsel = np.zeros((NCELL, 1), np.uint32)
    for xi in range(NEX):
        for b in range(NB):
            x = NEX * c + xi
            rsel[xi * NB + b, 0] = (2 * b + x // EH) * EH + (x % EH)
    eosel = np.zeros((128, E, NSL), np.uint32)
    for e in range(E):
        snd = e // NEX
        cell = (e % NEX) * NB + bc
        for sl in range(NSL):
            base = ((snd * NCELL + cell) * NSL + sl) * 128
            eosel[:, e, sl] = base + np.arange(128)
    p3sel = np.zeros((E, 1), np.uint32)
    for e in range(E):
        p3sel[e, 0] = (2 * bc + e // EH) * EH + (e % EH)
    return rsel, eosel, p3sel


def kernel_fused(L0_attn_x, L0_attn_W_qkv, L0_attn_W_proj, L0_ffn_gate_query,
                 L0_ffn_log_temp, L0_ffn_w1, L0_ffn_b1, L0_ffn_w2, L0_ffn_b2):
    import ml_dtypes
    T, D, H, E, CAP, DFF, B = _T, _D, _H, _E, _CAP, _DFF, _B
    x = np.ascontiguousarray(np.asarray(L0_attn_x, dtype=np.float32))
    wqkv = np.asarray(L0_attn_W_qkv, dtype=np.float32)
    wproj = np.ascontiguousarray(np.asarray(L0_attn_W_proj, dtype=np.float32))
    gq = np.asarray(L0_ffn_gate_query, dtype=np.float32)
    lt = np.asarray(L0_ffn_log_temp, dtype=np.float32).reshape(1, 1)
    w1 = np.asarray(L0_ffn_w1, dtype=np.float32)
    b1 = np.asarray(L0_ffn_b1, dtype=np.float32)
    w2 = np.asarray(L0_ffn_w2, dtype=np.float32)
    b2 = np.asarray(L0_ffn_b2, dtype=np.float32)
    HP = H // 2
    wq_p = np.empty((HP, D, 384), np.float32)
    for hp in range(HP):
        wq_p[hp, :, 0:128] = wqkv[:, 128 * hp:128 * hp + 128]
        wq_p[hp, :, 128:256] = wqkv[:, D + 128 * hp:D + 128 * hp + 128]
        wq_p[hp, :, 256:384] = wqkv[:, 2 * D + 128 * hp:2 * D + 128 * hp + 128]
    cst = _consts(T, D)
    gqT = np.ascontiguousarray(gq.T)
    identb = np.eye(128, dtype=ml_dtypes.bfloat16)

    nc = build_fused(T=T, D=D, H=H, E=E, CAP=CAP, DFF=DFF, NB=B)
    in_maps = []
    HPL = HP // 2
    EH = E // 2
    for c in range(NCORES):
        b = c // 2
        q = c % 2
        exs = [2 * c, 2 * c + 1]
        rsel, eosel, p3sel = _sel_tables(c, T=T, E=E, CAP=CAP, NB=B)
        perm = np.roll(np.arange(E), -EH * q)
        in_maps.append(dict(
            xT=np.ascontiguousarray(x[b].T),
            wqkv_p=np.ascontiguousarray(wq_p[q * HPL:(q + 1) * HPL]),
            wproj=wproj, gqT=np.ascontiguousarray(gqT[:, perm]), p3rsel=p3sel,
            lnsel=((np.arange(T // 256, dtype=np.uint32)[None, :] + (T // 256) * q)
                   * 128 + np.arange(128, dtype=np.uint32)[:, None]).copy(),
            ltemp=lt, identb=identb, rsel=rsel, eosel=eosel,
            osel=_osel_table(c, T, D),
            w1_i=np.ascontiguousarray(w1[exs]), w2_i=np.ascontiguousarray(w2[exs]),
            b1_i=np.ascontiguousarray(b1[exs]), b2_i=np.ascontiguousarray(b2[exs]),
            **cst))
    res = run_bass_kernel_spmd(nc, in_maps, core_ids=list(range(NCORES))).results
    half = T // 2
    out = np.stack(
        [np.concatenate([np.asarray(res[2 * b]["out_o"])[:half],
                         np.asarray(res[2 * b + 1]["out_o"])[half:]], axis=0)
         for b in range(B)], axis=0)
    return out.astype(np.float32)
